# version 07
# baseline (speedup 1.0000x reference)
import math
import contextlib
import numpy as np

import concourse.bass as bass
import concourse.mybir as mybir
from concourse.bass_utils import run_bass_kernel_spmd

F32 = mybir.dt.float32
BF16 = mybir.dt.bfloat16
AF = mybir.ActivationFunctionType
ALU = mybir.AluOpType
AX = mybir.AxisListType

N_CORES = 8
B_CORE = 4
S = 2048
D = 1024
H, DK = 16, 64
NE, FF = 8, 4096
NT = S // 128
ROPE_BASE = 6000.0
CSHIFT = 12.0
EPS = 1e-5
SHUF_MASK = list(range(16, 32)) + list(range(0, 16))

DEBUG = {}
SKIP_ATTN = False
SKIP_FFN = False
SKIP_QKV = False
SKIP_LN = False


class Ch:
    def __init__(self, es, nc, name):
        self.h = es.enter_context(nc.semaphore(name))
        self.n = 0

    def tick(self, k=1):
        self.n += k
        return self.n


def build_program(debug_names=()):
    nc = bass.Bass("TRN2", target_bir_lowering=False)

    x_in = nc.dram_tensor("L0_attn_x", [B_CORE, S, D], F32, kind="ExternalInput").ap()
    wqkv_in = nc.dram_tensor("L0_attn_W_qkv", [3, H, DK, D], F32, kind="ExternalInput").ap()
    wo_in = nc.dram_tensor("L0_attn_W_o", [D, D], F32, kind="ExternalInput").ap()
    rw_in = nc.dram_tensor("L0_ffn_router_w", [D, NE + 1], F32, kind="ExternalInput").ap()
    rb_in = nc.dram_tensor("L0_ffn_router_b", [NE + 1], F32, kind="ExternalInput").ap()
    w1_in = nc.dram_tensor("L0_ffn_w1", [NE, D, FF], F32, kind="ExternalInput").ap()
    w2_in = nc.dram_tensor("L0_ffn_w2", [NE, FF, D], F32, kind="ExternalInput").ap()
    out = nc.dram_tensor("out", [B_CORE, S, D], F32, kind="ExternalOutput").ap()

    wqkvT_d = nc.dram_tensor("wqkvT_d", [3, 8, 128, D], BF16).ap()
    woT_d = nc.dram_tensor("woT_d", [8, 128, D], BF16).ap()
    xnT_d = nc.dram_tensor("xnT_d", [B_CORE, 4, 8, 128, 512], BF16).ap()
    xntok_d = nc.dram_tensor("xntok_d", [B_CORE, S, D], BF16).ap()
    w1c_d = nc.dram_tensor("w1c_d", [NE, 8, 128, FF], BF16).ap()
    w2c_d = nc.dram_tensor("w2c_d", [NE, 2, FF, 512], BF16).ap()

    dbg = {}
    for name in debug_names:
        shp, dt = DEBUG[name]
        dbg[name] = nc.dram_tensor("dbg_" + name, shp, dt, kind="ExternalOutput").ap()

    es = contextlib.ExitStack()
    with es:
        sb = lambda name, shape, dt: es.enter_context(nc.sbuf_tensor(name, shape, dt))
        psum = lambda name, shape, dt=F32: es.enter_context(nc.psum_tensor(name, shape, dt))

        consts = sb("consts", [128, 8], F32)
        ones_bf = sb("ones_bf", [128, 128], BF16)
        ones_f = sb("ones_f", [128, 64], F32)
        ident_bf = sb("ident_bf", [128, 128], BF16)
        binmaskT = sb("binmaskT", [128, 128], BF16)
        rw_bf = sb("rw_bf", [128, 8 * 9], BF16)
        rb_bf = sb("rb_bf", [1, 9], BF16)
        gates = sb("gates", [128, NT * 9], F32)
        linv = sb("linv", [128, 1024], F32)
        lbs = sb("lbs", [128, 512], F32)
        stat = sb("stat", [128, 16], F32)
        cos_e = sb("cos_e", [128, S], BF16)
        sin_e = sb("sin_e", [128, S], BF16)
        irow = sb("irow", [1, 32], F32)
        wprep = sb("wprep", [128, D], BF16)
        wprep2 = sb("wprep2", [128, D], BF16)
        pbuf = sb("pbuf", [128, 4 * 512], BF16)
        outst = sb("outst", [128, 2 * D], F32)
        xst = sb("xst", [128, 2 * D], F32)
        lnsq = sb("lnsq", [128, D], BF16)
        xntok = sb("xntok", [128, 2 * D], BF16)

        base0 = (nc.sbuf_base + 31) & ~31
        DBYTES = {F32: 4, BF16: 2}

        def place(name, shape, dt, off):
            assert base0 + off + shape[1] * DBYTES[dt] <= nc.sbuf_top, name
            return nc.alloc_sbuf_tensor_at(name, shape, dt, offset=base0 + off)

        xnT = place("xnT", [128, 8 * S], BF16, 0)
        karena = place("karena", [128, 8 * S], BF16, 32768)
        varena = place("varena", [128, NT * (H * 65)], BF16, 65536)
        qarena = place("qarena", [128, 8 * 512], BF16, 98816)
        qsw = place("qsw", [128, S], BF16, 107008)
        wstr = place("wstr", [128, 8 * D], BF16, 111104)
        wstr2 = place("wstr2", [128, 8 * D], BF16, 127488)
        aoT = place("aoT", [128, 8 * 512], BF16, 143872)
        w1sb = place("w1sb", [128, 8 * FF], BF16, 0)
        hsb = place("hsb", [128, 32 * 512], BF16, 65536)
        accsb = place("accsb", [128, 8 * D], F32, 98304)
        xnf = place("xnf", [128, 2 * 4096], BF16, 131072)
        w2buf = place("w2buf", [128, 2 * 4096], BF16, 147456)
        rsc = place("rsc", [128, S], F32, 0)
        rsc2 = place("rsc2", [128, S], F32, 8192)
        rscb = place("rscb", [128, S], BF16, 16384)
        rrow = place("rrow", [128, S], F32, 20480)

        psA = psum("psA", [128, 512])
        psB = psum("psB", [128, 512])
        psT = psum("psT", [128, 1024], BF16)
        psAt0 = psum("psAt0", [128, 512])
        psAt1 = psum("psAt1", [128, 512])
        psPR = psum("psPR", [128, 1024])
        psR = psum("psR", [128, 512])

        cSY = Ch(es, nc, "cSY")
        cGD = Ch(es, nc, "cGD")
        cPE = Ch(es, nc, "cPE")
        cDV = Ch(es, nc, "cDV")
        cAC = Ch(es, nc, "cAC")
        cW1 = Ch(es, nc, "cW1")
        cW1s = Ch(es, nc, "cW1s")
        cPO = Ch(es, nc, "cPO")
        cXL = Ch(es, nc, "cXL")
        cXT = Ch(es, nc, "cXT")
        cXS = Ch(es, nc, "cXS")
        cXD = Ch(es, nc, "cXD")
        cOS = Ch(es, nc, "cOS")
        cXF0 = Ch(es, nc, "cXF0")
        cXF1 = Ch(es, nc, "cXF1")
        cW2a = Ch(es, nc, "cW2a")
        cW2b = Ch(es, nc, "cW2b")
        cW2sa = Ch(es, nc, "cW2sa")
        cW2sb = Ch(es, nc, "cW2sb")
        cWSa = Ch(es, nc, "cWSa")
        cWSb = Ch(es, nc, "cWSb")
        cW1C = Ch(es, nc, "cW1C")
        cW2Ca = Ch(es, nc, "cW2Ca")
        cW2Cb = Ch(es, nc, "cW2Cb")
        chans = [cSY, cGD, cPE, cDV, cAC, cW1, cW1s, cPO, cXL, cXT, cXS, cXD, cOS,
                 cXF0, cXF1, cW2a, cW2b, cWSa, cWSb, cW1C, cW2Ca, cW2Cb, cW2sa, cW2sb]
        chmap = {"sync": cSY, "gpsimd": cGD, "tensor": cPE, "vector": cDV, "scalar": cAC}

        Q = {e: [] for e in ("sync", "gpsimd", "vector", "scalar", "tensor")}

        def EMIT(eng, fn):
            Q[eng].append(fn)

        def INCC(eng, ch, fn, k=16):
            v = ch.tick(k)
            Q[eng].append(lambda e, f=fn, c=ch, kk=k: f(e).then_inc(c.h, kk))
            return v

        def INC(eng, fn, k=1):
            ch = chmap[eng]
            v = ch.tick(k)
            Q[eng].append(lambda e, f=fn, c=ch, kk=k: f(e).then_inc(c.h, kk))
            return v

        def WAIT(eng, ch, val):
            if val > 0:
                Q[eng].append(lambda e, c=ch, v=val: e.wait_ge(c.h, v))

        def VSTEP(fn):
            WAIT("vector", cDV, cDV.n)
            vv = INC("vector", fn)
            WAIT("vector", cDV, vv)
            return vv

        def barrier():
            vals = [(c, c.n) for c in chans]
            for eng in Q:
                for c, v in vals:
                    WAIT(eng, c, v)

        lnb = -math.log(ROPE_BASE) / 16.0
        for col, val in ((0, 0.125), (1, -CSHIFT), (2, lnb)):
            EMIT("vector", lambda v, c=col, vv=val: v.memset(consts[:, c:c + 1], vv))
        EMIT("vector", lambda v: v.memset(ones_bf[:], 1.0))
        EMIT("vector", lambda v: v.memset(ones_f[:], 1.0))

        for grp in range(4):
            EMIT("gpsimd", lambda g, g2=grp: g.iota(
                rsc[0:32, g2 * 128:(g2 + 1) * 128], pattern=[[1, 128]], base=-32 * g2,
                channel_multiplier=-1, allow_small_or_imprecise_dtypes=True))
        EMIT("gpsimd", lambda g: g.iota(rrow[0:1, :], pattern=[[1, S]], base=0,
                                        channel_multiplier=0,
                                        allow_small_or_imprecise_dtypes=True))
        cPO.tick()
        v_iota = cPO.n
        Q["gpsimd"].append(lambda g: g.iota(
            irow[0:1, :], pattern=[[1, 32]], base=0, channel_multiplier=0,
            allow_small_or_imprecise_dtypes=True).then_inc(cPO.h, 1))

        WAIT("vector", cPO, v_iota)

        VSTEP(lambda v: v.tensor_single_scalar(rsc2[0:32, 0:512], rsc[0:32, 0:512], 0.0, ALU.is_ge))
        VSTEP(lambda v: v.tensor_copy(rscb[0:32, 0:512], rsc2[0:32, 0:512]))
        VSTEP(lambda v: v.tensor_single_scalar(rsc2[0:32, 0:512], rsc[0:32, 0:512], 0.0, ALU.is_equal))
        v_mi = VSTEP(lambda v: v.tensor_copy(rscb[0:32, 512:1024], rsc2[0:32, 0:512]))
        WAIT("sync", cDV, v_mi)
        for grp in range(4):
            INC("sync", lambda sy, g2=grp: sy.dma_start(
                binmaskT[g2 * 32:(g2 + 1) * 32, :], rscb[0:32, g2 * 128:(g2 + 1) * 128]), 16)
        for grp in range(4):
            INC("sync", lambda sy, g2=grp: sy.dma_start(
                ident_bf[g2 * 32:(g2 + 1) * 32, :], rscb[0:32, 512 + g2 * 128:512 + (g2 + 1) * 128]), 16)
        s_mi = cSY.n

        INC("gpsimd", lambda g: g.dma_start(
            rw_bf[:].rearrange("p (c n) -> p c n", n=9),
            rw_in.rearrange("(c p) n -> p c n", p=128)), 16)
        g_rw = INC("gpsimd", lambda g: g.dma_start(rb_bf[:], rb_in[None, :]), 16)

        VSTEP(lambda v: v.tensor_single_scalar(rsc[0:1, 0:32], irow[0:1, :], 16.0, ALU.is_ge))
        v_if = VSTEP(lambda v: v.scalar_tensor_tensor(
            irow[0:1, :], rsc[0:1, 0:32], -16.0, irow[0:1, :], ALU.mult, ALU.add))
        WAIT("scalar", cDV, v_if)
        a_if = INC("scalar", lambda s: s.activation(irow[0:1, :], irow[0:1, :], AF.Exp,
                                                    scale=consts[0:1, 2:3]))

        TWO_PI = 2.0 * math.pi

        def emit_freqs():
            for i in range(4):
                p_mm = INC("tensor", lambda t, ii=i: t.matmul(
                    psR[0:32, 0:512], irow[0:1, :], rrow[0:1, ii * 512:(ii + 1) * 512],
                    start=True, stop=True))
                WAIT("vector", cPE, p_mm)
                v_cp = INC("vector", lambda v, ii=i: v.tensor_copy(
                    rsc[0:32, ii * 512:(ii + 1) * 512], psR[0:32, 0:512]))
                WAIT("tensor", cDV, v_cp)

        def rangered_sync():
            WAIT("vector", cDV, cDV.n)
            for j in range(8, -1, -1):
                mlt = TWO_PI * (2 ** j)
                VSTEP(lambda v, m2=mlt: v.tensor_single_scalar(rsc2[0:32, :], rsc[0:32, :], m2, ALU.is_ge))
                VSTEP(lambda v, m2=mlt: v.scalar_tensor_tensor(
                    rsc[0:32, :], rsc2[0:32, :], -m2, rsc[0:32, :], ALU.mult, ALU.add))
            VSTEP(lambda v: v.tensor_single_scalar(rsc2[0:32, :], rsc[0:32, :], math.pi, ALU.is_ge))
            return VSTEP(lambda v: v.scalar_tensor_tensor(
                rsc[0:32, :], rsc2[0:32, :], -TWO_PI, rsc[0:32, :], ALU.mult, ALU.add))

        WAIT("tensor", cAC, a_if)
        WAIT("tensor", cPO, v_iota)
        emit_freqs()
        v_rr = rangered_sync()
        WAIT("scalar", cDV, v_rr)
        a_sin = INC("scalar", lambda s: s.activation(rsc[0:32, :], rsc[0:32, :], AF.Sin))
        WAIT("vector", cAC, a_sin)
        WAIT("vector", cSY, s_mi)

        VSTEP(lambda v: v.tensor_scalar_mul(rsc[0:16, :], rsc[0:16, :], -1.0))
        v_sf = VSTEP(lambda v: v.tensor_copy(rscb[0:32, :], rsc[0:32, :]))
        VSTEP(lambda v: v.memset(sin_e[:], 0.0))
        v_sf = VSTEP(lambda v: v.memset(cos_e[:], 1.0))
        WAIT("sync", cDV, v_sf)
        for base in (0, 64):
            INC("sync", lambda sy, bb=base: sy.dma_start(sin_e[bb:bb + 32, :], rscb[0:32, :]), 16)
        s_sin = cSY.n
        WAIT("tensor", cDV, v_sf)
        emit_freqs()

        VSTEP(lambda v: v.tensor_scalar_add(rsc[0:32, :], rsc[0:32, :], math.pi / 2.0))
        v_cp2 = rangered_sync()
        WAIT("scalar", cDV, v_cp2)
        a_cos = INC("scalar", lambda s: s.activation(rsc[0:32, :], rsc[0:32, :], AF.Sin))
        WAIT("vector", cAC, a_cos)
        WAIT("vector", cSY, s_sin)
        v_cf = INC("vector", lambda v: v.tensor_copy(rscb[0:32, :], rsc[0:32, :]))
        WAIT("sync", cDV, v_cf)
        for base in (0, 64):
            INC("sync", lambda sy, bb=base: sy.dma_start(cos_e[bb:bb + 32, :], rscb[0:32, :]), 16)

        if "cos_e" in dbg:
            WAIT("sync", cSY, cSY.n)
            INC("sync", lambda sy: sy.dma_start(dbg["cos_e"], cos_e[:]), 16)
            INC("sync", lambda sy: sy.dma_start(dbg["sin_e"], sin_e[:]), 16)

        wqkv_r = wqkv_in.rearrange("q h d m -> q (h d) m").rearrange("q (r p) m -> q r p m", p=128)
        wo_r = wo_in.rearrange("(r p) m -> r p m", p=128)
        WAIT("tensor", cSY, s_mi)

        WAIT("gpsimd", cGD, cGD.n)

        def prep_weight(nq, dst_d):
            for q in range(nq):
                for r in range(8):
                    g_ld = INC("gpsimd", lambda g, qq=q, rr=r: g.dma_start(
                        wprep[:], wqkv_r[qq, rr] if nq > 1 else wo_r[rr]), 16)
                    WAIT("tensor", cGD, g_ld)
                    for mc in range(7):
                        EMIT("tensor", lambda t, m=mc: t.transpose(
                            psT[:, m * 128:(m + 1) * 128], wprep[:, m * 128:(m + 1) * 128],
                            ident_bf[:]))
                    p_tr = INC("tensor", lambda t: t.transpose(
                        psT[:, 7 * 128:8 * 128], wprep[:, 7 * 128:8 * 128], ident_bf[:]))
                    WAIT("vector", cPE, p_tr)
                    v_ev = INC("vector", lambda v: v.tensor_copy(wprep2[:], psT[:]))
                    WAIT("sync", cDV, v_ev)
                    for mc in range(8):
                        if nq > 1:
                            INC("sync", lambda sy, qq=q, rr=r, m=mc: sy.dma_start(
                                dst_d[qq, m, :, rr * 128:(rr + 1) * 128],
                                wprep2[:, m * 128:(m + 1) * 128]), 16)
                        else:
                            INC("sync", lambda sy, rr=r, m=mc: sy.dma_start(
                                dst_d[m, :, rr * 128:(rr + 1) * 128],
                                wprep2[:, m * 128:(m + 1) * 128]), 16)
                    s_st = cSY.n
                    WAIT("tensor", cSY, s_st)
                    WAIT("gpsimd", cSY, s_st)

        prep_weight(3, wqkvT_d)
        prep_weight(1, woT_d)
        barrier()

        for b in range(B_CORE):
            if SKIP_LN:
                barrier()
                continue
            xst_guard = {0: 0, 1: 0}
            xnT_ev = {}
            for ti in range(NT):
                sl = ti % 2
                s_x = INCC("sync", cXL, lambda sy, bb=b, tt=ti, s2=sl: sy.dma_start(
                    xst[:, s2 * D:(s2 + 1) * D], x_in[bb, tt * 128:(tt + 1) * 128, :]))
                xsl = xst[:, sl * D:(sl + 1) * D]
                xnsl = xntok[:, sl * D:(sl + 1) * D]
                WAIT("vector", cXL, s_x)
                v_1 = INC("vector", lambda v, s=xsl: v.reduce_sum(stat[:, 1:2], s, axis=AX.X))
                WAIT("scalar", cXL, s_x)
                a_1 = INC("scalar", lambda s, o=xsl: s.activation(
                    lnsq[:], o, AF.Square, accum_out=stat[:, 2:3]))
                WAIT("vector", cAC, a_1)
                VSTEP(lambda v: v.tensor_scalar_mul(stat[:, 1:2], stat[:, 1:2], 1.0 / D))
                VSTEP(lambda v: v.tensor_scalar(stat[:, 6:7], stat[:, 1:2], stat[:, 1:2],
                                                None, ALU.mult))
                VSTEP(lambda v: v.tensor_scalar(stat[:, 3:4], stat[:, 2:3], 1.0 / D, EPS,
                                                ALU.mult, ALU.add))
                v_2 = VSTEP(lambda v: v.tensor_sub(stat[:, 3:4], stat[:, 3:4], stat[:, 6:7]))
                WAIT("scalar", cDV, v_2)
                a_2 = INC("scalar", lambda s: s.activation(stat[:, 4:5], stat[:, 3:4], AF.Sqrt))
                WAIT("vector", cAC, a_2)
                VSTEP(lambda v: v.reciprocal(stat[:, 5:6], stat[:, 4:5]))
                v_3 = VSTEP(lambda v: v.tensor_scalar(stat[:, 6:7], stat[:, 1:2], stat[:, 5:6],
                                                      -1.0, ALU.mult, ALU.mult))
                WAIT("scalar", cDV, v_3)
                a_xn = INC("scalar", lambda s, o=xsl, xn=xnsl: s.activation(
                    xn, o, AF.Identity, scale=stat[:, 5:6], bias=stat[:, 6:7]))
                xst_guard[sl] = a_xn
                WAIT("sync", cAC, a_xn)
                s_xn = INCC("sync", cXS, lambda sy, bb=b, tt=ti, xn=xnsl: sy.dma_start(
                    xntok_d[bb, tt * 128:(tt + 1) * 128, :], xn))
                WAIT("scalar", cXS, s_xn)
                WAIT("tensor", cAC, a_xn)
                if ti >= 1:
                    WAIT("tensor", cDV, xnT_ev[ti - 1])
                for mc in range(7):
                    EMIT("tensor", lambda t, m=mc, xn=xnsl: t.transpose(
                        psT[:, m * 128:(m + 1) * 128], xn[:, m * 128:(m + 1) * 128], ident_bf[:]))
                p_tr = INC("tensor", lambda t, xn=xnsl: t.transpose(
                    psT[:, 7 * 128:8 * 128], xn[:, 7 * 128:8 * 128], ident_bf[:]))
                WAIT("vector", cPE, p_tr)

                def ev_xnT(v, tt=ti):
                    for m in range(7):
                        v.tensor_copy(xnT[:, m * S + tt * 128: m * S + (tt + 1) * 128],
                                      psT[:, m * 128:(m + 1) * 128])
                    return v.tensor_copy(xnT[:, 7 * S + tt * 128: 7 * S + (tt + 1) * 128],
                                         psT[:, 7 * 128:8 * 128])
                xnT_ev[ti] = INC("vector", ev_xnT)
            ln_done = cDV.n
            WAIT("sync", cDV, ln_done)
            for tsub in range(4):
                for mc in range(8):
                    INCC("sync", cXD, lambda sy, bb=b, ts=tsub, m=mc: sy.dma_start(
                        xnT_d[bb, ts, m], xnT[:, m * S + ts * 512: m * S + (ts + 1) * 512]))
            if "xnT" in dbg and b == 0:
                INC("sync", lambda sy: sy.dma_start(dbg["xnT"], xnT[:]), 16)
            if "xntok_d" in dbg and b == 0:
                INC("sync", lambda sy: sy.dma_start(dbg["xntok_d"], xntok_d[0]), 16)

            if SKIP_QKV:
                if True:
                    barrier()
                    continue
            g_wk = INCC("gpsimd", cWSb, lambda g: g.dma_start(
                wstr2[:].rearrange("p (c n) -> p c n", n=D),
                wqkvT_d[1].rearrange("c p n -> p c n")))
            WAIT("tensor", cWSb, g_wk)
            WAIT("tensor", cDV, ln_done)
            kev = {}
            for j in range(8):
                for tsl in range(4):
                    idx = j * 4 + tsl
                    ps = (psA, psB)[idx % 2]
                    if idx >= 2:
                        WAIT("tensor", cDV, kev[idx - 2])
                    for c in range(7):
                        EMIT("tensor", lambda t, jj=j, ts=tsl, cc=c, p=ps: t.matmul(
                            p[:, 0:512], wstr2[:, cc * D + jj * 128: cc * D + (jj + 1) * 128],
                            xnT[:, cc * S + ts * 512: cc * S + (ts + 1) * 512],
                            start=(cc == 0), stop=False))
                    p_k = INC("tensor", lambda t, jj=j, ts=tsl, p=ps: t.matmul(
                        p[:, 0:512], wstr2[:, 7 * D + jj * 128: 7 * D + (jj + 1) * 128],
                        xnT[:, 7 * S + ts * 512: 7 * S + (ts + 1) * 512],
                        start=False, stop=True))
                    WAIT("vector", cPE, p_k)
                    kev[idx] = INC("vector", lambda v, jj=j, ts=tsl, p=ps: v.tensor_copy(
                        karena[:, jj * S + ts * 512: jj * S + (ts + 1) * 512], p[:, 0:512]))

            g_wv = INCC("gpsimd", cWSa, lambda g: g.dma_start(
                wstr[:].rearrange("p (c n) -> p c n", n=D),
                wqkvT_d[2].rearrange("c p n -> p c n")))
            WAIT("tensor", cWSa, g_wv)
            WAIT("tensor", cGD, g_rw)
            vev = {}
            vme = {}
            for tc in range(NT):
                vlo = (psPR[:, 0:512], psA[:, 0:512])[tc % 2]
                vhi = (psPR[:, 512:1024], psB[:, 0:512])[tc % 2]
                if tc >= 2:
                    WAIT("tensor", cDV, vme[tc - 2])
                if tc >= 1:
                    WAIT("tensor", cAC, expR[tc - 1])
                for c in range(8):
                    EMIT("tensor", lambda t, cc=c, tt=tc, lo=vlo: t.matmul(
                        lo, xnT[:, cc * S + tt * 128: cc * S + (tt + 1) * 128],
                        wstr[:, cc * D: cc * D + 512],
                        start=(cc == 0), stop=(cc == 7)))
                    EMIT("tensor", lambda t, cc=c, tt=tc, hi=vhi: t.matmul(
                        hi, xnT[:, cc * S + tt * 128: cc * S + (tt + 1) * 128],
                        wstr[:, cc * D + 512: cc * D + 1024],
                        start=(cc == 0), stop=(cc == 7)))
                    EMIT("tensor", lambda t, cc=c, tt=tc: t.matmul(
                        psR[:, 0:9], xnT[:, cc * S + tt * 128: cc * S + (tt + 1) * 128],
                        rw_bf[:, cc * 9:(cc + 1) * 9],
                        start=(cc == 0), stop=False, skip_group_check=True))
                p_v = INC("tensor", lambda t: t.matmul(
                    psR[:, 0:9], ones_bf[0:1, :], rb_bf[0:1, :],
                    start=False, stop=True, skip_group_check=True))
                WAIT("vector", cPE, p_v)

                def v_ev_fn(v, tt=tc, lo=vlo, hi=vhi):
                    v.tensor_copy(
                        varena[:, tt * (H * 65): (tt + 1) * (H * 65)].rearrange(
                            "p (h n) -> p h n", h=H)[:, 0:8, 0:64],
                        lo.rearrange("p (h n) -> p h n", h=8))
                    v.tensor_copy(
                        varena[:, tt * (H * 65): (tt + 1) * (H * 65)].rearrange(
                            "p (h n) -> p h n", h=H)[:, 8:16, 0:64],
                        hi.rearrange("p (h n) -> p h n", h=8))
                    return v.reduce_max(stat[:, 8:9], psR[:, 0:9], axis=AX.X)
                v_m1 = INC("vector", v_ev_fn)
                vme[tc] = v_m1
                WAIT("scalar", cDV, v_m1)
                WAIT("vector", cDV, v_m1)
                gsl = slice(tc * 9, tc * 9 + 9)
                v_nm = INC("vector", lambda v: v.tensor_scalar_mul(stat[:, 10:11], stat[:, 8:9], -1.0))
                WAIT("scalar", cDV, v_nm)
                a_e = INC("scalar", lambda s, gs=gsl: s.activation(
                    gates[:, gs], psR[:, 0:9], AF.Exp, bias=stat[:, 10:11],
                    accum_out=stat[:, 9:10]))
                WAIT("vector", cAC, a_e)
                if tc == 0:
                    expR = {}
                expR[tc] = a_e

                gsl2 = gsl

                def t2step(fn):
                    vv2 = INC("vector", fn)
                    WAIT("vector", cDV, vv2)
                    return vv2
                t2step(lambda v, gs=gsl2: v.reduce_max(stat[:, 11:12], gates[:, gs], axis=AX.X))
                t2step(lambda v, gs=gsl2: v.tensor_scalar(
                    outst[:, 0:9], gates[:, gs], stat[:, 11:12], None, ALU.is_equal))
                t2step(lambda v, gs=gsl2: v.scalar_tensor_tensor(
                    outst[:, 0:9], outst[:, 0:9], -1e30, gates[:, gs], ALU.mult, ALU.add))
                t2step(lambda v: v.reduce_max(stat[:, 12:13], outst[:, 0:9], axis=AX.X))
                t2step(lambda v: v.reciprocal(stat[:, 13:14], stat[:, 9:10]))
                t2step(lambda v, gs=gsl2: v.tensor_scalar(
                    outst[:, 16:25], gates[:, gs], stat[:, 12:13], None, ALU.is_ge))
                t2step(lambda v, gs=gsl2: v.tensor_mul(gates[:, gs], gates[:, gs], outst[:, 16:25]))
                vev[tc] = t2step(lambda v, gs=gsl2: v.tensor_scalar_mul(
                    gates[:, gs], gates[:, gs], stat[:, 13:14]))
            v_ones = INC("vector", lambda v: v.memset(
                varena[:].rearrange("p (t n) -> p t n", n=65)[:, :, 64:65], 1.0))

            for j in range(8):
                ksl = karena[:, j * S:(j + 1) * S]

                v_sh = VSTEP(lambda v, k=ksl: v.stream_shuffle(qsw[:, 0:S], k, SHUF_MASK))

                VSTEP(lambda v, k=ksl: v.tensor_mul(qsw[:, 0:S], qsw[:, 0:S], sin_e[:]))
                VSTEP(lambda v, k=ksl: v.tensor_mul(k, k, cos_e[:]))
                INC("vector", lambda v, k=ksl: v.tensor_add(k, k, qsw[:, 0:S]))
            k_roped = cDV.n

            if "karena" in dbg and b == 0:
                WAIT("sync", cDV, k_roped)
                INC("sync", lambda sy: sy.dma_start(dbg["karena"], karena[:]), 16)
            if "varena" in dbg and b == 0:
                WAIT("sync", cDV, v_ones)
                INC("sync", lambda sy: sy.dma_start(dbg["varena"], varena[:]), 16)
            if "gates" in dbg and b == 0:
                WAIT("sync", cDV, cDV.n)
                INC("sync", lambda sy: sy.dma_start(dbg["gates"], gates[:]), 16)

            p_now = cPE.n
            WAIT("gpsimd", cPE, p_now)
            g_wq = INCC("gpsimd", cWSa, lambda g: g.dma_start(
                wstr[:].rearrange("p (c n) -> p c n", n=D),
                wqkvT_d[0].rearrange("c p n -> p c n")))
            g_wo = INCC("gpsimd", cWSb, lambda g: g.dma_start(
                wstr2[:].rearrange("p (c n) -> p c n", n=D),
                woT_d.rearrange("c p n -> p c n")))

            attev = {}
            for qg in range(0 if SKIP_ATTN else 4):
                q0 = qg * 512
                nkt = qg * 4 + 4
                WAIT("tensor", cWSa, g_wq)
                WAIT("tensor", cDV, k_roped)
                WAIT("tensor", cDV, v_ones)
                qev = {}
                for j in range(8):
                    ps = (psA, psB)[j % 2]
                    if j >= 2:
                        WAIT("tensor", cDV, qev[j - 2])
                    for c in range(7):
                        EMIT("tensor", lambda t, jj=j, cc=c, p=ps, qq=q0: t.matmul(
                            p[:, 0:512], wstr[:, cc * D + jj * 128: cc * D + (jj + 1) * 128],
                            xnT[:, cc * S + qq: cc * S + qq + 512],
                            start=(cc == 0), stop=False))
                    p_q = INC("tensor", lambda t, jj=j, p=ps, qq=q0: t.matmul(
                        p[:, 0:512], wstr[:, 7 * D + jj * 128: 7 * D + (jj + 1) * 128],
                        xnT[:, 7 * S + qq: 7 * S + qq + 512],
                        start=False, stop=True))
                    WAIT("vector", cPE, p_q)

                    v_t = INC("vector", lambda v, jj=j, p=ps: v.tensor_copy(
                        qarena[:, jj * 512:(jj + 1) * 512], p[:, 0:512]))
                    WAIT("vector", cDV, v_t)
                    v_t = VSTEP(lambda v, jj=j: v.stream_shuffle(
                        qsw[:, 0:512], qarena[:, jj * 512:(jj + 1) * 512], SHUF_MASK))

                    VSTEP(lambda v, qq=q0: v.tensor_mul(qsw[:, 0:512], qsw[:, 0:512],
                                                         sin_e[:, qq:qq + 512]))
                    VSTEP(lambda v, jj=j, qq=q0: v.tensor_mul(
                        qarena[:, jj * 512:(jj + 1) * 512],
                        qarena[:, jj * 512:(jj + 1) * 512], cos_e[:, qq:qq + 512]))
                    qev[j] = INC("vector", lambda v, jj=j: v.tensor_add(
                        qarena[:, jj * 512:(jj + 1) * 512],
                        qarena[:, jj * 512:(jj + 1) * 512], qsw[:, 0:512]))
                q_roped = cDV.n
                WAIT("tensor", cDV, q_roped)
                if "qarena" in dbg and b == 0 and qg == 0:
                    WAIT("sync", cDV, q_roped)
                    INC("sync", lambda sy: sy.dma_start(dbg["qarena"], qarena[:]), 16)

                if qg == 0:
                    WAIT("tensor", cDV, vme[NT - 1])
                sS = (psA[:, 0:512], psB[:, 0:512], psPR[:, 0:512], psPR[:, 512:1024])
                attb = (psAt0, psAt1)
                if qg == 0 and b == 0:
                    pass
                gtile = 0
                EVv = {}
                SVv = {}
                MKv = {}
                kt_of = {}

                def emit_S(head, kt, g):
                    off = 0 if kt < qg * 4 else (kt - qg * 4) * 128
                    kch, koff = head // 2, (head % 2) * 64
                    ps = sS[g % 4]
                    if g >= 4:
                        WAIT("tensor", cAC, EVv[g - 4])
                    SVv[g] = INC("tensor", lambda t, kk=kt, o=off, p=ps, kc=kch, ko=koff: t.matmul(
                        p[:, o:512] if o else p[:],
                        karena[ko:ko + 64, kc * S + kk * 128: kc * S + (kk + 1) * 128],
                        qarena[ko:ko + 64, kc * 512 + o: kc * 512 + 512],
                        start=True, stop=True, skip_group_check=True))
                    WAIT("scalar", cPE, SVv[g])
                    psl = pbuf[:, (g % 4) * 512:(g % 4) * 512 + 512]
                    EVv[g] = INC("scalar", lambda s2, o=off, p=ps, pb=psl: s2.activation(
                        pb[:, o:512], p[:, o:512], AF.Exp,
                        bias=consts[:, 1:2], scale=consts[:, 0:1]))
                    if kt >= qg * 4:
                        WAIT("vector", cAC, EVv[g])
                        MKv[g] = INC("vector", lambda v, pb=psl, o=off: v.tensor_mul(
                            pb[:, o:o + 128], pb[:, o:o + 128], binmaskT[:]))
                    kt_of[g] = (head, kt, off, psl)

                def emit_aV(g, prev_attev):
                    head, kt, off, psl = kt_of[g]
                    a = attb[head % 2]
                    if g in MKv:
                        WAIT("tensor", cDV, MKv[g])
                    else:
                        WAIT("tensor", cAC, EVv[g])
                    if kt == 0 and prev_attev is not None:
                        WAIT("tensor", cDV, prev_attev)
                    return INC("tensor", lambda t, kk=kt, o=off, pb=psl, hh=head, a2=a, nn=nkt: t.matmul(
                        a2[0:65, o:512],
                        varena[:, kk * (H * 65) + hh * 65: kk * (H * 65) + (hh + 1) * 65],
                        pb[:, o:512],
                        start=(kk == 0), stop=(kk == nn - 1), skip_group_check=True))

                attev_prev = {0: None, 1: None}
                pend_epi = None

                def emit_epilogue(p_attA, p_attB, kch, interleave_S=None):
                    nonlocal lb2_prev
                    isl = list(interleave_S or [])

                    def isl_pop():
                        if isl:
                            emit_S(*isl.pop(0))
                    ev = {}
                    rec = {}
                    for x, (p_att, ko) in enumerate(((p_attA, 0), (p_attB, 64))):
                        WAIT("vector", cPE, p_att)
                        EMIT("vector", lambda v, a=attb[x], kc=kch, ko2=ko: v.tensor_copy(
                            aoT[ko2:ko2 + 64, kc * 512:(kc + 1) * 512], a[0:64, 0:512]))
                        rec[x] = INC("vector", lambda v, a=attb[x], xx=x: v.reciprocal(
                            linv[64:65, xx * 512:(xx + 1) * 512], a[64:65, 0:512]))
                        WAIT("vector", cDV, rec[x])
                        ev[x] = rec[x]
                        isl_pop()
                    WAIT("tensor", cDV, rec[0])
                    WAIT("tensor", cDV, lb2_prev)
                    EMIT("tensor", lambda t: t.matmul(
                        psR[0:64, 0:512], ones_f[64:65, 0:64], linv[64:65, 0:512],
                        start=True, stop=True))
                    isl_pop()
                    WAIT("tensor", cDV, rec[1])
                    k1b = INC("tensor", lambda t: t.matmul(
                        psR[64:128, 0:512], ones_f[64:65, 0:64], linv[64:65, 512:1024],
                        start=True, stop=True))
                    isl_pop()
                    WAIT("vector", cPE, k1b)
                    lb2 = INC("vector", lambda v: v.tensor_copy(lbs[:], psR[:, 0:512]))
                    WAIT("vector", cDV, lb2)
                    lb2_prev = INC("vector", lambda v, kc=kch: v.tensor_mul(
                        aoT[:, kc * 512:(kc + 1) * 512], aoT[:, kc * 512:(kc + 1) * 512],
                        lbs[:]))
                    while isl:
                        isl_pop()
                    return ev

                lb2_prev = 0
                attev_prev = {0: None, 1: None}
                pend_epi = None


                for hp in range(8):
                    hA, hB = 2 * hp, 2 * hp + 1
                    tiles = []
                    for kt in range(nkt):
                        tiles.append((hA, kt))
                        tiles.append((hB, kt))
                    n = len(tiles)
                    gbase = gtile
                    pre = min(4, n)
                    if pend_epi is not None:
                        pA, pB, pk = pend_epi
                        epi_tiles = [(tiles[i][0], tiles[i][1], gbase + i) for i in range(pre)]
                        ev = emit_epilogue(pA, pB, pk, epi_tiles)
                        attev_prev = {0: ev[0], 1: ev[1]}
                        pend_epi = None
                    else:
                        for i in range(pre):
                            emit_S(tiles[i][0], tiles[i][1], gbase + i)
                    p_attA = p_attB = None
                    for i in range(n):
                        g = gbase + i
                        if i + 4 < n:
                            pass
                        avv = emit_aV(g, attev_prev[tiles[i][0] % 2] if tiles[i][1] == 0 else None)
                        if tiles[i][1] == nkt - 1:
                            if tiles[i][0] == hA:
                                p_attA = avv
                            else:
                                p_attB = avv
                        if i + 4 < n:
                            emit_S(tiles[i + 4][0], tiles[i + 4][1], gbase + i + 4)
                    gtile += n
                    pend_epi = (p_attA, p_attB, hp)
                pA, pB, pk = pend_epi
                ev = emit_epilogue(pA, pB, pk, None)
                attev_prev = {0: ev[0], 1: ev[1]}
                ao_done = cDV.n
                WAIT("tensor", cDV, ao_done)
                WAIT("tensor", cWSb, g_wo)
                if "aoT" in dbg and b == 0 and qg == 0:
                    WAIT("sync", cDV, ao_done)
                    INC("sync", lambda sy: sy.dma_start(dbg["aoT"], aoT[:]), 16)

                for qc in range(4):
                    ti = qg * 4 + qc
                    sl = qc % 2
                    s_x2 = INCC("sync", cXL, lambda sy, bb=b, tt=ti, s2=sl: sy.dma_start(
                        xst[:, s2 * D:(s2 + 1) * D], x_in[bb, tt * 128:(tt + 1) * 128, :]))
                    for hc in range(8):
                        EMIT("tensor", lambda t, hh=hc, q=qc: t.matmul(
                            psPR[:, 0:512], aoT[:, hh * 512 + q * 128: hh * 512 + (q + 1) * 128],
                            wstr2[:, hh * D: hh * D + 512],
                            start=(hh == 0), stop=(hh == 7)))
                        if hc < 7:
                            EMIT("tensor", lambda t, hh=hc, q=qc: t.matmul(
                                psPR[:, 512:1024], aoT[:, hh * 512 + q * 128: hh * 512 + (q + 1) * 128],
                                wstr2[:, hh * D + 512: hh * D + 1024],
                                start=(hh == 0), stop=False))
                    p_pr = INC("tensor", lambda t, q=qc: t.matmul(
                        psPR[:, 512:1024], aoT[:, 7 * 512 + q * 128: 7 * 512 + (q + 1) * 128],
                        wstr2[:, 7 * D + 512: 7 * D + 1024],
                        start=False, stop=True))
                    WAIT("vector", cPE, p_pr)
                    WAIT("vector", cXL, s_x2)
                    v_o = INC("vector", lambda v, s2=sl: v.scalar_tensor_tensor(
                        outst[:, s2 * D:(s2 + 1) * D], psPR[:], 1.0,
                        xst[:, s2 * D:(s2 + 1) * D], ALU.bypass, ALU.add))
                    WAIT("tensor", cDV, v_o)
                    WAIT("sync", cDV, v_o)
                    s_o = INCC("sync", cOS, lambda sy, bb=b, tt=ti, s2=sl: sy.dma_start(
                        out[bb, tt * 128:(tt + 1) * 128, :], outst[:, s2 * D:(s2 + 1) * D]))
                    WAIT("vector", cOS, s_o)

            if "attpart" in dbg and b == 0:
                WAIT("sync", cSY, cSY.n)
                INC("sync", lambda sy: sy.dma_start(dbg["attpart"], out[0]), 16)

            barrier()
            if b == 0:
                w1cw = {}
                w2cw = {}
                nc._w1cw, nc._w2cw = w1cw, w2cw
            else:
                w1cw, w2cw = nc._w1cw, nc._w2cw
            xnf_guard = {0: 0, 1: 0}
            w2_guard = {0: 0, 1: 0}
            w2_sguard = {0: 0, 1: 0}
            w1_guard = 0
            for th in range(0 if SKIP_FFN else 2):
                for e in range(NE):
                    first = (b == 0 and th == 0)
                    if first:
                        WAIT("gpsimd", cPE, w1_guard)
                        WAIT("gpsimd", cW1C, cW1C.n)
                    if not first:
                        WAIT("sync", cPE, w1_guard)
                        WAIT("sync", cW1C, cW1C.n)
                    w1ch = cW1 if first else cW1s
                    for mc in range(8):
                        if first:
                            INCC("gpsimd", cW1, lambda g, ee=e, m=mc: g.dma_start(
                                w1sb[:, m * FF:(m + 1) * FF],
                                w1_in[ee, m * 128:(m + 1) * 128, :]))
                        else:
                            INCC("sync", cW1s, lambda sy, ee=e, m=mc: sy.dma_start(
                                w1sb[:, m * FF:(m + 1) * FF], w1c_d[ee, m]))
                    g_w1 = w1ch.n
                    if first:
                        WAIT("sync", cW1, g_w1)
                        for _unused in ():
                            pass
                        for mc in range(8):
                            INCC("sync", cW1C, lambda sy, ee=e, m=mc: sy.dma_start(
                                w1c_d[ee, m], w1sb[:, m * FF:(m + 1) * FF]))
                        w1cw[e] = cW1C.n
                    for ts in range(2):
                        tsub = th * 2 + ts
                        sl = ts
                        WAIT("sync", cPE, xnf_guard[sl])
                        cXF = (cXF0, cXF1)[sl]
                        s_xn = INCC("sync", cXF, lambda sy, bb=b, t2=tsub, s2=sl: sy.dma_start(
                            xnf[:, s2 * 4096:(s2 + 1) * 4096].rearrange("p (c n) -> p c n", n=512),
                            xnT_d[bb, t2].rearrange("c p n -> p c n")))
                        WAIT("tensor", w1ch, g_w1)
                        WAIT("tensor", cXF, s_xn)
                        gel = {}
                        for fc in range(32):
                            ps = (psA, psB, psR)[fc % 3]
                            if fc >= 3:
                                WAIT("tensor", cAC, gel[fc - 3])
                            for c in range(7):
                                EMIT("tensor", lambda t, f=fc, cc=c, p=ps, s2=sl: t.matmul(
                                    p[:, 0:512],
                                    w1sb[:, cc * FF + f * 128: cc * FF + (f + 1) * 128],
                                    xnf[:, s2 * 4096 + cc * 512: s2 * 4096 + (cc + 1) * 512],
                                    start=(cc == 0), stop=False))
                            p_1 = INC("tensor", lambda t, f=fc, p=ps, s2=sl: t.matmul(
                                p[:, 0:512], w1sb[:, 7 * FF + f * 128: 7 * FF + (f + 1) * 128],
                                xnf[:, s2 * 4096 + 7 * 512: s2 * 4096 + 8 * 512],
                                start=False, stop=True))
                            WAIT("scalar", cPE, p_1)
                            gel[fc] = INC("scalar", lambda s, f=fc, p=ps: s.activation(
                                hsb[:, f * 512:(f + 1) * 512], p[:, 0:512], AF.Gelu))
                        xnf_guard[sl] = cPE.n
                        h_done = cAC.n
                        if "hsb" in dbg and b == 0 and th == 0 and e == 0 and ts == 0:
                            WAIT("sync", cAC, h_done)
                            INC("sync", lambda sy: sy.dma_start(dbg["hsb"], hsb[:]), 16)
                        for ddh in range(2):
                            for fq in range(4):
                                w2sl = fq % 2
                                cW2 = (cW2a, cW2b)[w2sl]
                                cW2C = (cW2Ca, cW2Cb)[w2sl]
                                if first and ts == 0:
                                    WAIT("gpsimd", cPE, w2_guard[w2sl])
                                    WAIT("gpsimd", cW2C, w2_sguard[w2sl])
                                    g_w2 = INCC("gpsimd", cW2, lambda g, ee=e, d2=ddh, f2=fq, s3=w2sl: g.dma_start(
                                        w2buf[:, s3 * 4096:(s3 + 1) * 4096].rearrange(
                                            "p (c n) -> p c n", n=512),
                                        w2_in[ee, f2 * 1024:(f2 + 1) * 1024,
                                              d2 * 512:(d2 + 1) * 512].rearrange(
                                            "(c p) n -> p c n", p=128)))
                                    WAIT("sync", cW2, g_w2)
                                    s_w2 = INCC("sync", cW2C, lambda sy, ee=e, d2=ddh, f2=fq, s3=w2sl: sy.dma_start(
                                        w2c_d[ee, d2, f2 * 1024:(f2 + 1) * 1024, :].rearrange(
                                            "(c p) n -> p c n", p=128),
                                        w2buf[:, s3 * 4096:(s3 + 1) * 4096].rearrange(
                                            "p (c n) -> p c n", n=512)))
                                    w2_sguard[w2sl] = s_w2
                                    WAIT("tensor", cW2, g_w2)
                                else:
                                    cW2s = (cW2sa, cW2sb)[w2sl]
                                    wa, wb = w2cw.get(e, (0, 0))
                                    WAIT("sync", cW2Ca, wa)
                                    WAIT("sync", cW2Cb, wb)
                                    WAIT("sync", cPE, w2_guard[w2sl])
                                    g_w2 = INCC("sync", cW2s, lambda sy, ee=e, d2=ddh, f2=fq, s3=w2sl: sy.dma_start(
                                        w2buf[:, s3 * 4096:(s3 + 1) * 4096].rearrange(
                                            "p (c n) -> p c n", n=512),
                                        w2c_d[ee, d2, f2 * 1024:(f2 + 1) * 1024, :].rearrange(
                                            "(c p) n -> p c n", p=128)))
                                    WAIT("tensor", cW2s, g_w2)
                                WAIT("tensor", cAC, gel[fq * 8 + 7])
                                for tc in range(4):
                                    for fc in range(8):
                                        fg = fq * 8 + fc

                                        def p2mm(t, f2=fq, f=fc, t3=tc, s3=w2sl, fgv=fg):
                                            ysl = (psAt0[:, 0:512], psAt1[:, 0:512],
                                                   psPR[:, 0:512], psPR[:, 512:1024])[t3]
                                            return t.matmul(
                                                ysl,
                                                hsb[:, fgv * 512 + t3 * 128: fgv * 512 + (t3 + 1) * 128],
                                                w2buf[:, s3 * 4096 + f * 512: s3 * 4096 + (f + 1) * 512],
                                                start=(fgv == 0), stop=(fgv == 31),
                                                skip_group_check=True)
                                        if tc == 3 and fc == 7:
                                            INC("tensor", p2mm)
                                        else:
                                            EMIT("tensor", p2mm)
                                w2_guard[w2sl] = cPE.n
                            p_y = cPE.n
                            WAIT("vector", cPE, p_y)

                            def y_ev(v, d2=ddh, t2=tsub, t4=ts, ee=e):
                                last = None
                                for tc in range(4):
                                    tct = t2 * 4 + tc
                                    tca = t4 * 4 + tc
                                    ysl = (psAt0[:, 0:512], psAt1[:, 0:512],
                                           psPR[:, 0:512], psPR[:, 512:1024])[tc]
                                    asl = accsb[:, tca * D + d2 * 512: tca * D + d2 * 512 + 512]
                                    gcol = gates[:, tct * 9 + ee: tct * 9 + ee + 1]
                                    if ee == 0:
                                        last = v.tensor_scalar(asl, ysl, gcol, None, ALU.mult)
                                    else:
                                        last = v.scalar_tensor_tensor(asl, ysl, gcol, asl,
                                                                      ALU.mult, ALU.add)
                                return last
                            v_y = INC("vector", y_ev)
                            WAIT("tensor", cDV, v_y)
                        if first and ts == 0:
                            w2cw[e] = (cW2Ca.n, cW2Cb.n)
                        if ts == 1:
                            w1_guard = xnf_guard[sl]
                for tc8 in range(8):
                    tct = th * 8 + tc8
                    sl = tc8 % 2
                    s_ol = INCC("sync", cXL, lambda sy, bb=b, tt=tct, s2=sl: sy.dma_start(
                        xst[:, s2 * D:(s2 + 1) * D], out[bb, tt * 128:(tt + 1) * 128, :]))
                    s_xt = INCC("sync", cXT, lambda sy, bb=b, tt=tct, s2=sl: sy.dma_start(
                        xntok[:, s2 * D:(s2 + 1) * D], xntok_d[bb, tt * 128:(tt + 1) * 128, :]))
                    WAIT("vector", cXT, s_xt)
                    WAIT("vector", cXL, s_ol)

                    VSTEP(lambda v, t5=tct, t6=tc8, s2=sl: v.scalar_tensor_tensor(
                        outst[:, s2 * D:(s2 + 1) * D], xntok[:, s2 * D:(s2 + 1) * D],
                        gates[:, t5 * 9 + 8: t5 * 9 + 9], accsb[:, t6 * D:(t6 + 1) * D],
                        ALU.mult, ALU.add))
                    v_mg = INC("vector", lambda v, s2=sl: v.tensor_add(
                        outst[:, s2 * D:(s2 + 1) * D], outst[:, s2 * D:(s2 + 1) * D],
                        xst[:, s2 * D:(s2 + 1) * D]))
                    WAIT("sync", cDV, v_mg)
                    s_o2 = INCC("sync", cOS, lambda sy, bb=b, tt=tct, s2=sl: sy.dma_start(
                        out[bb, tt * 128:(tt + 1) * 128, :], outst[:, s2 * D:(s2 + 1) * D]))
                    WAIT("vector", cOS, s_o2)
            barrier()

        for c in chans:
            WAIT("sync", c, c.n)

        blk = es.enter_context(nc.Block())

        def mk(name):
            def body(e):
                for f in Q[name]:
                    f(e)
            return body
        blk.sync(mk("sync"))
        blk.gpsimd(mk("gpsimd"))
        blk.vector(mk("vector"))
        blk.scalar(mk("scalar"))
        blk.tensor(mk("tensor"))
    return nc


def kernel(L0_attn_x, L0_attn_W_qkv, L0_attn_W_o, L0_ffn_router_w, L0_ffn_router_b,
           L0_ffn_w1, L0_ffn_w2, _debug_names=(), _nc=None):
    x = np.ascontiguousarray(np.asarray(L0_attn_x))
    shards = [x[i * B_CORE:(i + 1) * B_CORE] for i in range(N_CORES)]
    nc = _nc if _nc is not None else build_program(_debug_names)
    base = {
        "L0_attn_W_qkv": np.ascontiguousarray(np.asarray(L0_attn_W_qkv)),
        "L0_attn_W_o": np.ascontiguousarray(np.asarray(L0_attn_W_o)),
        "L0_ffn_router_w": np.ascontiguousarray(np.asarray(L0_ffn_router_w)),
        "L0_ffn_router_b": np.ascontiguousarray(np.asarray(L0_ffn_router_b)),
        "L0_ffn_w1": np.ascontiguousarray(np.asarray(L0_ffn_w1)),
        "L0_ffn_w2": np.ascontiguousarray(np.asarray(L0_ffn_w2)),
    }
    in_maps = [dict(base, L0_attn_x=shards[i]) for i in range(N_CORES)]
    res = run_bass_kernel_spmd(nc, in_maps, core_ids=list(range(N_CORES)))
    outs = [np.asarray(res.results[i]["out"]) for i in range(N_CORES)]
    full = np.concatenate(outs, axis=0)
    if _debug_names:
        return full, res
    return full


# version 08
# speedup vs baseline: 1.0669x; 1.0669x over previous
import math
import contextlib
import numpy as np

import concourse.bass as bass
import concourse.mybir as mybir
from concourse.bass_utils import run_bass_kernel_spmd

F32 = mybir.dt.float32
BF16 = mybir.dt.bfloat16
AF = mybir.ActivationFunctionType
ALU = mybir.AluOpType
AX = mybir.AxisListType

N_CORES = 8
B_CORE = 4
S = 2048
D = 1024
H, DK = 16, 64
NE, FF = 8, 4096
NT = S // 128
ROPE_BASE = 6000.0
CSHIFT = 12.0
EPS = 1e-5
SHUF_MASK = list(range(16, 32)) + list(range(0, 16))

DEBUG = {}
SKIP_ATTN = False
SKIP_FFN = False
SKIP_QKV = False
SKIP_LN = False


class Ch:
    def __init__(self, es, nc, name):
        self.h = es.enter_context(nc.semaphore(name))
        self.n = 0

    def tick(self, k=1):
        self.n += k
        return self.n


def build_program(debug_names=()):
    nc = bass.Bass("TRN2", target_bir_lowering=False)

    x_in = nc.dram_tensor("L0_attn_x", [B_CORE, S, D], F32, kind="ExternalInput").ap()
    wqkv_in = nc.dram_tensor("L0_attn_W_qkv", [3, H, DK, D], F32, kind="ExternalInput").ap()
    wo_in = nc.dram_tensor("L0_attn_W_o", [D, D], F32, kind="ExternalInput").ap()
    rw_in = nc.dram_tensor("L0_ffn_router_w", [D, NE + 1], F32, kind="ExternalInput").ap()
    rb_in = nc.dram_tensor("L0_ffn_router_b", [NE + 1], F32, kind="ExternalInput").ap()
    w1_in = nc.dram_tensor("L0_ffn_w1", [NE, D, FF], F32, kind="ExternalInput").ap()
    w2_in = nc.dram_tensor("L0_ffn_w2", [NE, FF, D], F32, kind="ExternalInput").ap()
    out = nc.dram_tensor("out", [B_CORE, S, D], F32, kind="ExternalOutput").ap()

    wqkvT_d = nc.dram_tensor("wqkvT_d", [3, 8, 128, D], BF16).ap()
    woT_d = nc.dram_tensor("woT_d", [8, 128, D], BF16).ap()
    xnT_d = nc.dram_tensor("xnT_d", [B_CORE, 4, 8, 128, 512], BF16).ap()
    xntok_d = nc.dram_tensor("xntok_d", [B_CORE, S, D], BF16).ap()
    w1c_d = nc.dram_tensor("w1c_d", [NE, 8, 128, FF], BF16).ap()
    w2c_d = nc.dram_tensor("w2c_d", [NE, 2, FF, 512], BF16).ap()

    dbg = {}
    for name in debug_names:
        shp, dt = DEBUG[name]
        dbg[name] = nc.dram_tensor("dbg_" + name, shp, dt, kind="ExternalOutput").ap()

    es = contextlib.ExitStack()
    with es:
        sb = lambda name, shape, dt: es.enter_context(nc.sbuf_tensor(name, shape, dt))
        psum = lambda name, shape, dt=F32: es.enter_context(nc.psum_tensor(name, shape, dt))

        consts = sb("consts", [128, 8], F32)
        ones_bf = sb("ones_bf", [128, 128], BF16)
        ones_f = sb("ones_f", [128, 64], F32)
        ident_bf = sb("ident_bf", [128, 128], BF16)
        binmaskT = sb("binmaskT", [128, 128], BF16)
        maskadd = sb("maskadd", [128, 128], BF16)
        rw_bf = sb("rw_bf", [128, 8 * 9], BF16)
        rb_bf = sb("rb_bf", [1, 9], BF16)
        gates = sb("gates", [128, NT * 9], F32)
        linv = sb("linv", [128, 1024], F32)
        lbs = sb("lbs", [128, 512], F32)
        stat = sb("stat", [128, 16], F32)
        cos_e = sb("cos_e", [128, S], BF16)
        sin_e = sb("sin_e", [128, S], BF16)
        irow = sb("irow", [1, 32], F32)
        wprep = sb("wprep", [128, D], BF16)
        wprep2 = sb("wprep2", [128, D], BF16)
        pbuf = sb("pbuf", [128, 4 * 512], BF16)
        outst = sb("outst", [128, 2 * D], F32)
        xst = sb("xst", [128, 2 * D], F32)
        lnsq = sb("lnsq", [128, D], BF16)
        xntok = sb("xntok", [128, 2 * D], BF16)

        base0 = (nc.sbuf_base + 31) & ~31
        DBYTES = {F32: 4, BF16: 2}

        def place(name, shape, dt, off):
            assert base0 + off + shape[1] * DBYTES[dt] <= nc.sbuf_top, name
            return nc.alloc_sbuf_tensor_at(name, shape, dt, offset=base0 + off)

        xnT = place("xnT", [128, 8 * S], BF16, 0)
        karena = place("karena", [128, 8 * S], BF16, 32768)
        varena = place("varena", [128, NT * (H * 65)], BF16, 65536)
        qarena = place("qarena", [128, 8 * 512], BF16, 98816)
        qsw = place("qsw", [128, S], BF16, 107008)
        wstr = place("wstr", [128, 8 * D], BF16, 111104)
        wstr2 = place("wstr2", [128, 8 * D], BF16, 127488)
        aoT = place("aoT", [128, 8 * 512], BF16, 143872)
        w1sb = place("w1sb", [128, 8 * FF], BF16, 0)
        hsb = place("hsb", [128, 32 * 512], BF16, 65536)
        accsb = place("accsb", [128, 8 * D], F32, 98304)
        xnf = place("xnf", [128, 2 * 4096], BF16, 131072)
        w2buf = place("w2buf", [128, 2 * 4096], BF16, 147456)
        rsc = place("rsc", [128, S], F32, 0)
        rsc2 = place("rsc2", [128, S], F32, 8192)
        rscb = place("rscb", [128, S], BF16, 16384)
        rrow = place("rrow", [128, S], F32, 20480)

        psA = psum("psA", [128, 512])
        psB = psum("psB", [128, 512])
        psT = psum("psT", [128, 1024], BF16)
        psAt0 = psum("psAt0", [128, 512])
        psAt1 = psum("psAt1", [128, 512])
        psPR = psum("psPR", [128, 1024])
        psR = psum("psR", [128, 512])

        cSY = Ch(es, nc, "cSY")
        cGD = Ch(es, nc, "cGD")
        cPE = Ch(es, nc, "cPE")
        cDV = Ch(es, nc, "cDV")
        cAC = Ch(es, nc, "cAC")
        cW1 = Ch(es, nc, "cW1")
        cW1s = Ch(es, nc, "cW1s")
        cPO = Ch(es, nc, "cPO")
        cXL = Ch(es, nc, "cXL")
        cXT = Ch(es, nc, "cXT")
        cXS = Ch(es, nc, "cXS")
        cXD = Ch(es, nc, "cXD")
        cOS = Ch(es, nc, "cOS")
        cXF0 = Ch(es, nc, "cXF0")
        cXF1 = Ch(es, nc, "cXF1")
        cW2a = Ch(es, nc, "cW2a")
        cW2b = Ch(es, nc, "cW2b")
        cW2sa = Ch(es, nc, "cW2sa")
        cW2sb = Ch(es, nc, "cW2sb")
        cWSa = Ch(es, nc, "cWSa")
        cWSb = Ch(es, nc, "cWSb")
        cW1C = Ch(es, nc, "cW1C")
        cW2Ca = Ch(es, nc, "cW2Ca")
        cW2Cb = Ch(es, nc, "cW2Cb")
        chans = [cSY, cGD, cPE, cDV, cAC, cW1, cW1s, cPO, cXL, cXT, cXS, cXD, cOS,
                 cXF0, cXF1, cW2a, cW2b, cWSa, cWSb, cW1C, cW2Ca, cW2Cb, cW2sa, cW2sb]
        chmap = {"sync": cSY, "gpsimd": cGD, "tensor": cPE, "vector": cDV, "scalar": cAC}

        Q = {e: [] for e in ("sync", "gpsimd", "vector", "scalar", "tensor")}

        def EMIT(eng, fn):
            Q[eng].append(fn)

        def INCC(eng, ch, fn, k=16):
            v = ch.tick(k)
            Q[eng].append(lambda e, f=fn, c=ch, kk=k: f(e).then_inc(c.h, kk))
            return v

        def INC(eng, fn, k=1):
            ch = chmap[eng]
            v = ch.tick(k)
            Q[eng].append(lambda e, f=fn, c=ch, kk=k: f(e).then_inc(c.h, kk))
            return v

        def WAIT(eng, ch, val):
            if val > 0:
                Q[eng].append(lambda e, c=ch, v=val: e.wait_ge(c.h, v))

        def VSTEP(fn):
            WAIT("vector", cDV, cDV.n)
            vv = INC("vector", fn)
            WAIT("vector", cDV, vv)
            return vv

        def barrier():
            vals = [(c, c.n) for c in chans]
            for eng in Q:
                for c, v in vals:
                    WAIT(eng, c, v)

        lnb = -math.log(ROPE_BASE) / 16.0
        for col, val in ((0, 0.125), (1, -CSHIFT), (2, lnb)):
            EMIT("vector", lambda v, c=col, vv=val: v.memset(consts[:, c:c + 1], vv))
        EMIT("vector", lambda v: v.memset(ones_bf[:], 1.0))
        EMIT("vector", lambda v: v.memset(ones_f[:], 1.0))

        for grp in range(4):
            EMIT("gpsimd", lambda g, g2=grp: g.iota(
                rsc[0:32, g2 * 128:(g2 + 1) * 128], pattern=[[1, 128]], base=-32 * g2,
                channel_multiplier=-1, allow_small_or_imprecise_dtypes=True))
        EMIT("gpsimd", lambda g: g.iota(rrow[0:1, :], pattern=[[1, S]], base=0,
                                        channel_multiplier=0,
                                        allow_small_or_imprecise_dtypes=True))
        cPO.tick()
        v_iota = cPO.n
        Q["gpsimd"].append(lambda g: g.iota(
            irow[0:1, :], pattern=[[1, 32]], base=0, channel_multiplier=0,
            allow_small_or_imprecise_dtypes=True).then_inc(cPO.h, 1))

        WAIT("vector", cPO, v_iota)

        VSTEP(lambda v: v.tensor_single_scalar(rsc2[0:32, 0:512], rsc[0:32, 0:512], 0.0, ALU.is_ge))
        VSTEP(lambda v: v.tensor_copy(rscb[0:32, 0:512], rsc2[0:32, 0:512]))
        VSTEP(lambda v: v.tensor_single_scalar(rsc2[0:32, 0:512], rsc[0:32, 0:512], 0.0, ALU.is_equal))
        v_mi = VSTEP(lambda v: v.tensor_copy(rscb[0:32, 512:1024], rsc2[0:32, 0:512]))
        WAIT("sync", cDV, v_mi)
        for grp in range(4):
            INC("sync", lambda sy, g2=grp: sy.dma_start(
                binmaskT[g2 * 32:(g2 + 1) * 32, :], rscb[0:32, g2 * 128:(g2 + 1) * 128]), 16)
        for grp in range(4):
            INC("sync", lambda sy, g2=grp: sy.dma_start(
                ident_bf[g2 * 32:(g2 + 1) * 32, :], rscb[0:32, 512 + g2 * 128:512 + (g2 + 1) * 128]), 16)
        s_mi = cSY.n

        WAIT("vector", cSY, s_mi)
        VSTEP(lambda v: v.tensor_scalar(maskadd[:], binmaskT[:], -1.0, 1600.0,
                                        ALU.add, ALU.mult))
        INC("gpsimd", lambda g: g.dma_start(
            rw_bf[:].rearrange("p (c n) -> p c n", n=9),
            rw_in.rearrange("(c p) n -> p c n", p=128)), 16)
        g_rw = INC("gpsimd", lambda g: g.dma_start(rb_bf[:], rb_in[None, :]), 16)

        VSTEP(lambda v: v.tensor_single_scalar(rsc[0:1, 0:32], irow[0:1, :], 16.0, ALU.is_ge))
        v_if = VSTEP(lambda v: v.scalar_tensor_tensor(
            irow[0:1, :], rsc[0:1, 0:32], -16.0, irow[0:1, :], ALU.mult, ALU.add))
        WAIT("scalar", cDV, v_if)
        a_if = INC("scalar", lambda s: s.activation(irow[0:1, :], irow[0:1, :], AF.Exp,
                                                    scale=consts[0:1, 2:3]))

        TWO_PI = 2.0 * math.pi

        def emit_freqs():
            for i in range(4):
                p_mm = INC("tensor", lambda t, ii=i: t.matmul(
                    psR[0:32, 0:512], irow[0:1, :], rrow[0:1, ii * 512:(ii + 1) * 512],
                    start=True, stop=True))
                WAIT("vector", cPE, p_mm)
                v_cp = INC("vector", lambda v, ii=i: v.tensor_copy(
                    rsc[0:32, ii * 512:(ii + 1) * 512], psR[0:32, 0:512]))
                WAIT("tensor", cDV, v_cp)

        def rangered_sync():
            WAIT("vector", cDV, cDV.n)
            for j in range(8, -1, -1):
                mlt = TWO_PI * (2 ** j)
                VSTEP(lambda v, m2=mlt: v.tensor_single_scalar(rsc2[0:32, :], rsc[0:32, :], m2, ALU.is_ge))
                VSTEP(lambda v, m2=mlt: v.scalar_tensor_tensor(
                    rsc[0:32, :], rsc2[0:32, :], -m2, rsc[0:32, :], ALU.mult, ALU.add))
            VSTEP(lambda v: v.tensor_single_scalar(rsc2[0:32, :], rsc[0:32, :], math.pi, ALU.is_ge))
            return VSTEP(lambda v: v.scalar_tensor_tensor(
                rsc[0:32, :], rsc2[0:32, :], -TWO_PI, rsc[0:32, :], ALU.mult, ALU.add))

        WAIT("tensor", cAC, a_if)
        WAIT("tensor", cPO, v_iota)
        emit_freqs()
        v_rr = rangered_sync()
        WAIT("scalar", cDV, v_rr)
        a_sin = INC("scalar", lambda s: s.activation(rsc[0:32, :], rsc[0:32, :], AF.Sin))
        WAIT("vector", cAC, a_sin)
        WAIT("vector", cSY, s_mi)

        VSTEP(lambda v: v.tensor_scalar_mul(rsc[0:16, :], rsc[0:16, :], -1.0))
        v_sf = VSTEP(lambda v: v.tensor_copy(rscb[0:32, :], rsc[0:32, :]))
        VSTEP(lambda v: v.memset(sin_e[:], 0.0))
        v_sf = VSTEP(lambda v: v.memset(cos_e[:], 1.0))
        WAIT("sync", cDV, v_sf)
        for base in (0, 64):
            INC("sync", lambda sy, bb=base: sy.dma_start(sin_e[bb:bb + 32, :], rscb[0:32, :]), 16)
        s_sin = cSY.n
        WAIT("tensor", cDV, v_sf)
        emit_freqs()

        VSTEP(lambda v: v.tensor_scalar_add(rsc[0:32, :], rsc[0:32, :], math.pi / 2.0))
        v_cp2 = rangered_sync()
        WAIT("scalar", cDV, v_cp2)
        a_cos = INC("scalar", lambda s: s.activation(rsc[0:32, :], rsc[0:32, :], AF.Sin))
        WAIT("vector", cAC, a_cos)
        WAIT("vector", cSY, s_sin)
        v_cf = INC("vector", lambda v: v.tensor_copy(rscb[0:32, :], rsc[0:32, :]))
        WAIT("sync", cDV, v_cf)
        for base in (0, 64):
            INC("sync", lambda sy, bb=base: sy.dma_start(cos_e[bb:bb + 32, :], rscb[0:32, :]), 16)

        if "cos_e" in dbg:
            WAIT("sync", cSY, cSY.n)
            INC("sync", lambda sy: sy.dma_start(dbg["cos_e"], cos_e[:]), 16)
            INC("sync", lambda sy: sy.dma_start(dbg["sin_e"], sin_e[:]), 16)

        wqkv_r = wqkv_in.rearrange("q h d m -> q (h d) m").rearrange("q (r p) m -> q r p m", p=128)
        wo_r = wo_in.rearrange("(r p) m -> r p m", p=128)
        WAIT("tensor", cSY, s_mi)

        WAIT("gpsimd", cGD, cGD.n)

        def prep_weight(nq, dst_d):
            for q in range(nq):
                for r in range(8):
                    g_ld = INC("gpsimd", lambda g, qq=q, rr=r: g.dma_start(
                        wprep[:], wqkv_r[qq, rr] if nq > 1 else wo_r[rr]), 16)
                    WAIT("tensor", cGD, g_ld)
                    for mc in range(7):
                        EMIT("tensor", lambda t, m=mc: t.transpose(
                            psT[:, m * 128:(m + 1) * 128], wprep[:, m * 128:(m + 1) * 128],
                            ident_bf[:]))
                    p_tr = INC("tensor", lambda t: t.transpose(
                        psT[:, 7 * 128:8 * 128], wprep[:, 7 * 128:8 * 128], ident_bf[:]))
                    WAIT("vector", cPE, p_tr)
                    v_ev = INC("vector", lambda v: v.tensor_copy(wprep2[:], psT[:]))
                    WAIT("sync", cDV, v_ev)
                    for mc in range(8):
                        if nq > 1:
                            INC("sync", lambda sy, qq=q, rr=r, m=mc: sy.dma_start(
                                dst_d[qq, m, :, rr * 128:(rr + 1) * 128],
                                wprep2[:, m * 128:(m + 1) * 128]), 16)
                        else:
                            INC("sync", lambda sy, rr=r, m=mc: sy.dma_start(
                                dst_d[m, :, rr * 128:(rr + 1) * 128],
                                wprep2[:, m * 128:(m + 1) * 128]), 16)
                    s_st = cSY.n
                    WAIT("tensor", cSY, s_st)
                    WAIT("gpsimd", cSY, s_st)

        prep_weight(3, wqkvT_d)
        prep_weight(1, woT_d)
        barrier()

        for b in range(B_CORE):
            if SKIP_LN:
                barrier()
                continue
            xst_guard = {0: 0, 1: 0}
            xnT_ev = {}
            for ti in range(NT):
                sl = ti % 2
                s_x = INCC("sync", cXL, lambda sy, bb=b, tt=ti, s2=sl: sy.dma_start(
                    xst[:, s2 * D:(s2 + 1) * D], x_in[bb, tt * 128:(tt + 1) * 128, :]))
                xsl = xst[:, sl * D:(sl + 1) * D]
                xnsl = xntok[:, sl * D:(sl + 1) * D]
                WAIT("vector", cXL, s_x)
                v_1 = INC("vector", lambda v, s=xsl: v.reduce_sum(stat[:, 1:2], s, axis=AX.X))
                WAIT("scalar", cXL, s_x)
                a_1 = INC("scalar", lambda s, o=xsl: s.activation(
                    lnsq[:], o, AF.Square, accum_out=stat[:, 2:3]))
                WAIT("vector", cAC, a_1)
                VSTEP(lambda v: v.tensor_scalar_mul(stat[:, 1:2], stat[:, 1:2], 1.0 / D))
                VSTEP(lambda v: v.tensor_scalar(stat[:, 6:7], stat[:, 1:2], stat[:, 1:2],
                                                None, ALU.mult))
                VSTEP(lambda v: v.tensor_scalar(stat[:, 3:4], stat[:, 2:3], 1.0 / D, EPS,
                                                ALU.mult, ALU.add))
                v_2 = VSTEP(lambda v: v.tensor_sub(stat[:, 3:4], stat[:, 3:4], stat[:, 6:7]))
                WAIT("scalar", cDV, v_2)
                a_2 = INC("scalar", lambda s: s.activation(stat[:, 4:5], stat[:, 3:4], AF.Sqrt))
                WAIT("vector", cAC, a_2)
                VSTEP(lambda v: v.reciprocal(stat[:, 5:6], stat[:, 4:5]))
                v_3 = VSTEP(lambda v: v.tensor_scalar(stat[:, 6:7], stat[:, 1:2], stat[:, 5:6],
                                                      -1.0, ALU.mult, ALU.mult))
                WAIT("scalar", cDV, v_3)
                a_xn = INC("scalar", lambda s, o=xsl, xn=xnsl: s.activation(
                    xn, o, AF.Identity, scale=stat[:, 5:6], bias=stat[:, 6:7]))
                xst_guard[sl] = a_xn
                WAIT("sync", cAC, a_xn)
                s_xn = INCC("sync", cXS, lambda sy, bb=b, tt=ti, xn=xnsl: sy.dma_start(
                    xntok_d[bb, tt * 128:(tt + 1) * 128, :], xn))
                WAIT("scalar", cXS, s_xn)
                WAIT("tensor", cAC, a_xn)
                if ti >= 1:
                    WAIT("tensor", cDV, xnT_ev[ti - 1])
                for mc in range(7):
                    EMIT("tensor", lambda t, m=mc, xn=xnsl: t.transpose(
                        psT[:, m * 128:(m + 1) * 128], xn[:, m * 128:(m + 1) * 128], ident_bf[:]))
                p_tr = INC("tensor", lambda t, xn=xnsl: t.transpose(
                    psT[:, 7 * 128:8 * 128], xn[:, 7 * 128:8 * 128], ident_bf[:]))
                WAIT("vector", cPE, p_tr)

                def ev_xnT(v, tt=ti):
                    for m in range(7):
                        v.tensor_copy(xnT[:, m * S + tt * 128: m * S + (tt + 1) * 128],
                                      psT[:, m * 128:(m + 1) * 128])
                    return v.tensor_copy(xnT[:, 7 * S + tt * 128: 7 * S + (tt + 1) * 128],
                                         psT[:, 7 * 128:8 * 128])
                xnT_ev[ti] = INC("vector", ev_xnT)
            ln_done = cDV.n
            WAIT("sync", cDV, ln_done)
            for tsub in range(4):
                for mc in range(8):
                    INCC("sync", cXD, lambda sy, bb=b, ts=tsub, m=mc: sy.dma_start(
                        xnT_d[bb, ts, m], xnT[:, m * S + ts * 512: m * S + (ts + 1) * 512]))
            if "xnT" in dbg and b == 0:
                INC("sync", lambda sy: sy.dma_start(dbg["xnT"], xnT[:]), 16)
            if "xntok_d" in dbg and b == 0:
                INC("sync", lambda sy: sy.dma_start(dbg["xntok_d"], xntok_d[0]), 16)

            if SKIP_QKV:
                if True:
                    barrier()
                    continue
            g_wk = INCC("gpsimd", cWSb, lambda g: g.dma_start(
                wstr2[:].rearrange("p (c n) -> p c n", n=D),
                wqkvT_d[1].rearrange("c p n -> p c n")))
            WAIT("tensor", cWSb, g_wk)
            WAIT("tensor", cDV, ln_done)
            kev = {}
            for j in range(8):
                for tsl in range(4):
                    idx = j * 4 + tsl
                    ps = (psA, psB)[idx % 2]
                    if idx >= 2:
                        WAIT("tensor", cDV, kev[idx - 2])
                    for c in range(7):
                        EMIT("tensor", lambda t, jj=j, ts=tsl, cc=c, p=ps: t.matmul(
                            p[:, 0:512], wstr2[:, cc * D + jj * 128: cc * D + (jj + 1) * 128],
                            xnT[:, cc * S + ts * 512: cc * S + (ts + 1) * 512],
                            start=(cc == 0), stop=False))
                    p_k = INC("tensor", lambda t, jj=j, ts=tsl, p=ps: t.matmul(
                        p[:, 0:512], wstr2[:, 7 * D + jj * 128: 7 * D + (jj + 1) * 128],
                        xnT[:, 7 * S + ts * 512: 7 * S + (ts + 1) * 512],
                        start=False, stop=True))
                    WAIT("vector", cPE, p_k)
                    kev[idx] = INC("vector", lambda v, jj=j, ts=tsl, p=ps: v.tensor_copy(
                        karena[:, jj * S + ts * 512: jj * S + (ts + 1) * 512], p[:, 0:512]))

            g_wv = INCC("gpsimd", cWSa, lambda g: g.dma_start(
                wstr[:].rearrange("p (c n) -> p c n", n=D),
                wqkvT_d[2].rearrange("c p n -> p c n")))
            WAIT("tensor", cWSa, g_wv)
            WAIT("tensor", cGD, g_rw)
            vev = {}
            vme = {}
            for tc in range(NT):
                vlo = (psPR[:, 0:512], psA[:, 0:512])[tc % 2]
                vhi = (psPR[:, 512:1024], psB[:, 0:512])[tc % 2]
                if tc >= 2:
                    WAIT("tensor", cDV, vme[tc - 2])
                if tc >= 1:
                    WAIT("tensor", cAC, expR[tc - 1])
                for c in range(8):
                    EMIT("tensor", lambda t, cc=c, tt=tc, lo=vlo: t.matmul(
                        lo, xnT[:, cc * S + tt * 128: cc * S + (tt + 1) * 128],
                        wstr[:, cc * D: cc * D + 512],
                        start=(cc == 0), stop=(cc == 7)))
                    EMIT("tensor", lambda t, cc=c, tt=tc, hi=vhi: t.matmul(
                        hi, xnT[:, cc * S + tt * 128: cc * S + (tt + 1) * 128],
                        wstr[:, cc * D + 512: cc * D + 1024],
                        start=(cc == 0), stop=(cc == 7)))
                    EMIT("tensor", lambda t, cc=c, tt=tc: t.matmul(
                        psR[:, 0:9], xnT[:, cc * S + tt * 128: cc * S + (tt + 1) * 128],
                        rw_bf[:, cc * 9:(cc + 1) * 9],
                        start=(cc == 0), stop=False, skip_group_check=True))
                p_v = INC("tensor", lambda t: t.matmul(
                    psR[:, 0:9], ones_bf[0:1, :], rb_bf[0:1, :],
                    start=False, stop=True, skip_group_check=True))
                WAIT("vector", cPE, p_v)

                def v_ev_fn(v, tt=tc, lo=vlo, hi=vhi):
                    v.tensor_copy(
                        varena[:, tt * (H * 65): (tt + 1) * (H * 65)].rearrange(
                            "p (h n) -> p h n", h=H)[:, 0:8, 0:64],
                        lo.rearrange("p (h n) -> p h n", h=8))
                    v.tensor_copy(
                        varena[:, tt * (H * 65): (tt + 1) * (H * 65)].rearrange(
                            "p (h n) -> p h n", h=H)[:, 8:16, 0:64],
                        hi.rearrange("p (h n) -> p h n", h=8))
                    return v.reduce_max(stat[:, 8:9], psR[:, 0:9], axis=AX.X)
                v_m1 = INC("vector", v_ev_fn)
                vme[tc] = v_m1
                WAIT("scalar", cDV, v_m1)
                WAIT("vector", cDV, v_m1)
                gsl = slice(tc * 9, tc * 9 + 9)
                v_nm = INC("vector", lambda v: v.tensor_scalar_mul(stat[:, 10:11], stat[:, 8:9], -1.0))
                WAIT("scalar", cDV, v_nm)
                a_e = INC("scalar", lambda s, gs=gsl: s.activation(
                    gates[:, gs], psR[:, 0:9], AF.Exp, bias=stat[:, 10:11],
                    accum_out=stat[:, 9:10]))
                WAIT("vector", cAC, a_e)
                if tc == 0:
                    expR = {}
                expR[tc] = a_e

                gsl2 = gsl

                def t2step(fn):
                    vv2 = INC("vector", fn)
                    WAIT("vector", cDV, vv2)
                    return vv2
                t2step(lambda v, gs=gsl2: v.reduce_max(stat[:, 11:12], gates[:, gs], axis=AX.X))
                t2step(lambda v, gs=gsl2: v.tensor_scalar(
                    outst[:, 0:9], gates[:, gs], stat[:, 11:12], None, ALU.is_equal))
                t2step(lambda v, gs=gsl2: v.scalar_tensor_tensor(
                    outst[:, 0:9], outst[:, 0:9], -1e30, gates[:, gs], ALU.mult, ALU.add))
                t2step(lambda v: v.reduce_max(stat[:, 12:13], outst[:, 0:9], axis=AX.X))
                t2step(lambda v: v.reciprocal(stat[:, 13:14], stat[:, 9:10]))
                t2step(lambda v, gs=gsl2: v.tensor_scalar(
                    outst[:, 16:25], gates[:, gs], stat[:, 12:13], None, ALU.is_ge))
                t2step(lambda v, gs=gsl2: v.tensor_mul(gates[:, gs], gates[:, gs], outst[:, 16:25]))
                vev[tc] = t2step(lambda v, gs=gsl2: v.tensor_scalar_mul(
                    gates[:, gs], gates[:, gs], stat[:, 13:14]))
            v_ones = INC("vector", lambda v: v.memset(
                varena[:].rearrange("p (t n) -> p t n", n=65)[:, :, 64:65], 1.0))

            for j in range(8):
                ksl = karena[:, j * S:(j + 1) * S]

                v_sh = VSTEP(lambda v, k=ksl: v.stream_shuffle(qsw[:, 0:S], k, SHUF_MASK))

                VSTEP(lambda v, k=ksl: v.tensor_mul(qsw[:, 0:S], qsw[:, 0:S], sin_e[:]))
                VSTEP(lambda v, k=ksl: v.tensor_mul(k, k, cos_e[:]))
                INC("vector", lambda v, k=ksl: v.tensor_add(k, k, qsw[:, 0:S]))
            k_roped = cDV.n

            if "karena" in dbg and b == 0:
                WAIT("sync", cDV, k_roped)
                INC("sync", lambda sy: sy.dma_start(dbg["karena"], karena[:]), 16)
            if "varena" in dbg and b == 0:
                WAIT("sync", cDV, v_ones)
                INC("sync", lambda sy: sy.dma_start(dbg["varena"], varena[:]), 16)
            if "gates" in dbg and b == 0:
                WAIT("sync", cDV, cDV.n)
                INC("sync", lambda sy: sy.dma_start(dbg["gates"], gates[:]), 16)

            p_now = cPE.n
            WAIT("gpsimd", cPE, p_now)
            g_wq = INCC("gpsimd", cWSa, lambda g: g.dma_start(
                wstr[:].rearrange("p (c n) -> p c n", n=D),
                wqkvT_d[0].rearrange("c p n -> p c n")))
            g_wo = INCC("gpsimd", cWSb, lambda g: g.dma_start(
                wstr2[:].rearrange("p (c n) -> p c n", n=D),
                woT_d.rearrange("c p n -> p c n")))

            attev = {}
            for qg in range(0 if SKIP_ATTN else 4):
                q0 = qg * 512
                nkt = qg * 4 + 4
                WAIT("tensor", cWSa, g_wq)
                WAIT("tensor", cDV, k_roped)
                WAIT("tensor", cDV, v_ones)
                qev = {}
                for j in range(8):
                    ps = (psA, psB)[j % 2]
                    if j >= 2:
                        WAIT("tensor", cDV, qev[j - 2])
                    for c in range(7):
                        EMIT("tensor", lambda t, jj=j, cc=c, p=ps, qq=q0: t.matmul(
                            p[:, 0:512], wstr[:, cc * D + jj * 128: cc * D + (jj + 1) * 128],
                            xnT[:, cc * S + qq: cc * S + qq + 512],
                            start=(cc == 0), stop=False))
                    p_q = INC("tensor", lambda t, jj=j, p=ps, qq=q0: t.matmul(
                        p[:, 0:512], wstr[:, 7 * D + jj * 128: 7 * D + (jj + 1) * 128],
                        xnT[:, 7 * S + qq: 7 * S + qq + 512],
                        start=False, stop=True))
                    WAIT("vector", cPE, p_q)

                    v_t = INC("vector", lambda v, jj=j, p=ps: v.tensor_copy(
                        qarena[:, jj * 512:(jj + 1) * 512], p[:, 0:512]))
                    WAIT("vector", cDV, v_t)
                    v_t = VSTEP(lambda v, jj=j: v.stream_shuffle(
                        qsw[:, 0:512], qarena[:, jj * 512:(jj + 1) * 512], SHUF_MASK))

                    VSTEP(lambda v, qq=q0: v.tensor_mul(qsw[:, 0:512], qsw[:, 0:512],
                                                         sin_e[:, qq:qq + 512]))
                    VSTEP(lambda v, jj=j, qq=q0: v.tensor_mul(
                        qarena[:, jj * 512:(jj + 1) * 512],
                        qarena[:, jj * 512:(jj + 1) * 512], cos_e[:, qq:qq + 512]))
                    qev[j] = INC("vector", lambda v, jj=j: v.tensor_add(
                        qarena[:, jj * 512:(jj + 1) * 512],
                        qarena[:, jj * 512:(jj + 1) * 512], qsw[:, 0:512]))
                q_roped = cDV.n
                WAIT("tensor", cDV, q_roped)
                if "qarena" in dbg and b == 0 and qg == 0:
                    WAIT("sync", cDV, q_roped)
                    INC("sync", lambda sy: sy.dma_start(dbg["qarena"], qarena[:]), 16)

                if qg == 0:
                    WAIT("tensor", cDV, vme[NT - 1])
                sS = (psA[:, 0:512], psB[:, 0:512], psPR[:, 0:512], psPR[:, 512:1024])
                attb = (psAt0, psAt1)
                if qg == 0 and b == 0:
                    pass
                gtile = 0
                EVv = {}
                SVv = {}
                MKv = {}
                kt_of = {}

                def emit_S(head, kt, g):
                    off = 0 if kt < qg * 4 else (kt - qg * 4) * 128
                    kch, koff = head // 2, (head % 2) * 64
                    ps = sS[g % 4]
                    if g >= 4:
                        WAIT("tensor", cAC, EVv[g - 4])
                    diag = kt >= qg * 4
                    if diag:
                        EMIT("tensor", lambda t, kk=kt, o=off, p=ps, kc=kch, ko=koff: t.matmul(
                            p[:, o:512] if o else p[:],
                            karena[ko:ko + 64, kc * S + kk * 128: kc * S + (kk + 1) * 128],
                            qarena[ko:ko + 64, kc * 512 + o: kc * 512 + 512],
                            start=True, stop=False, skip_group_check=True))
                        SVv[g] = INC("tensor", lambda t, o=off, p=ps: t.matmul(
                            p[:, o:o + 128], ident_bf[:], maskadd[:],
                            start=False, stop=True, skip_group_check=True))
                    else:
                        SVv[g] = INC("tensor", lambda t, kk=kt, o=off, p=ps, kc=kch, ko=koff: t.matmul(
                            p[:, o:512] if o else p[:],
                            karena[ko:ko + 64, kc * S + kk * 128: kc * S + (kk + 1) * 128],
                            qarena[ko:ko + 64, kc * 512 + o: kc * 512 + 512],
                            start=True, stop=True, skip_group_check=True))
                    WAIT("scalar", cPE, SVv[g])
                    psl = pbuf[:, (g % 4) * 512:(g % 4) * 512 + 512]
                    EVv[g] = INC("scalar", lambda s2, o=off, p=ps, pb=psl: s2.activation(
                        pb[:, o:512], p[:, o:512], AF.Exp,
                        bias=consts[:, 1:2], scale=consts[:, 0:1]))
                    kt_of[g] = (head, kt, off, psl)

                def emit_aV(g, prev_attev):
                    head, kt, off, psl = kt_of[g]
                    a = attb[head % 2]
                    WAIT("tensor", cAC, EVv[g])
                    if kt == 0 and prev_attev is not None:
                        WAIT("tensor", cDV, prev_attev)
                    return INC("tensor", lambda t, kk=kt, o=off, pb=psl, hh=head, a2=a, nn=nkt: t.matmul(
                        a2[0:65, o:512],
                        varena[:, kk * (H * 65) + hh * 65: kk * (H * 65) + (hh + 1) * 65],
                        pb[:, o:512],
                        start=(kk == 0), stop=(kk == nn - 1), skip_group_check=True))

                attev_prev = {0: None, 1: None}
                pend_epi = None

                def emit_epilogue(p_attA, p_attB, kch, interleave_S=None):
                    nonlocal lb2_prev
                    isl = list(interleave_S or [])

                    def isl_pop():
                        if isl:
                            emit_S(*isl.pop(0))
                    ev = {}
                    rec = {}
                    for x, (p_att, ko) in enumerate(((p_attA, 0), (p_attB, 64))):
                        WAIT("vector", cPE, p_att)
                        EMIT("vector", lambda v, a=attb[x], kc=kch, ko2=ko: v.tensor_copy(
                            aoT[ko2:ko2 + 64, kc * 512:(kc + 1) * 512], a[0:64, 0:512]))
                        rec[x] = INC("vector", lambda v, a=attb[x], xx=x: v.reciprocal(
                            linv[64:65, xx * 512:(xx + 1) * 512], a[64:65, 0:512]))
                        WAIT("vector", cDV, rec[x])
                        ev[x] = rec[x]
                        isl_pop()
                    WAIT("tensor", cDV, rec[0])
                    WAIT("tensor", cDV, lb2_prev)
                    EMIT("tensor", lambda t: t.matmul(
                        psR[0:64, 0:512], ones_f[64:65, 0:64], linv[64:65, 0:512],
                        start=True, stop=True))
                    isl_pop()
                    WAIT("tensor", cDV, rec[1])
                    k1b = INC("tensor", lambda t: t.matmul(
                        psR[64:128, 0:512], ones_f[64:65, 0:64], linv[64:65, 512:1024],
                        start=True, stop=True))
                    isl_pop()
                    WAIT("vector", cPE, k1b)
                    lb2 = INC("vector", lambda v: v.tensor_copy(lbs[:], psR[:, 0:512]))
                    WAIT("vector", cDV, lb2)
                    lb2_prev = INC("vector", lambda v, kc=kch: v.tensor_mul(
                        aoT[:, kc * 512:(kc + 1) * 512], aoT[:, kc * 512:(kc + 1) * 512],
                        lbs[:]))
                    while isl:
                        isl_pop()
                    return ev

                lb2_prev = 0
                attev_prev = {0: None, 1: None}
                pend_epi = None


                for hp in range(8):
                    hA, hB = 2 * hp, 2 * hp + 1
                    tiles = []
                    for kt in range(nkt):
                        tiles.append((hA, kt))
                        tiles.append((hB, kt))
                    n = len(tiles)
                    gbase = gtile
                    pre = min(4, n)
                    if pend_epi is not None:
                        pA, pB, pk = pend_epi
                        epi_tiles = [(tiles[i][0], tiles[i][1], gbase + i) for i in range(pre)]
                        ev = emit_epilogue(pA, pB, pk, epi_tiles)
                        attev_prev = {0: ev[0], 1: ev[1]}
                        pend_epi = None
                    else:
                        for i in range(pre):
                            emit_S(tiles[i][0], tiles[i][1], gbase + i)
                    p_attA = p_attB = None
                    for i in range(n):
                        g = gbase + i
                        if i + 4 < n:
                            pass
                        avv = emit_aV(g, attev_prev[tiles[i][0] % 2] if tiles[i][1] == 0 else None)
                        if tiles[i][1] == nkt - 1:
                            if tiles[i][0] == hA:
                                p_attA = avv
                            else:
                                p_attB = avv
                        if i + 4 < n:
                            emit_S(tiles[i + 4][0], tiles[i + 4][1], gbase + i + 4)
                    gtile += n
                    pend_epi = (p_attA, p_attB, hp)
                pA, pB, pk = pend_epi
                ev = emit_epilogue(pA, pB, pk, None)
                attev_prev = {0: ev[0], 1: ev[1]}
                ao_done = cDV.n
                WAIT("tensor", cDV, ao_done)
                WAIT("tensor", cWSb, g_wo)
                if "aoT" in dbg and b == 0 and qg == 0:
                    WAIT("sync", cDV, ao_done)
                    INC("sync", lambda sy: sy.dma_start(dbg["aoT"], aoT[:]), 16)

                for qc in range(4):
                    ti = qg * 4 + qc
                    sl = qc % 2
                    s_x2 = INCC("sync", cXL, lambda sy, bb=b, tt=ti, s2=sl: sy.dma_start(
                        xst[:, s2 * D:(s2 + 1) * D], x_in[bb, tt * 128:(tt + 1) * 128, :]))
                    for hc in range(8):
                        EMIT("tensor", lambda t, hh=hc, q=qc: t.matmul(
                            psPR[:, 0:512], aoT[:, hh * 512 + q * 128: hh * 512 + (q + 1) * 128],
                            wstr2[:, hh * D: hh * D + 512],
                            start=(hh == 0), stop=(hh == 7)))
                        if hc < 7:
                            EMIT("tensor", lambda t, hh=hc, q=qc: t.matmul(
                                psPR[:, 512:1024], aoT[:, hh * 512 + q * 128: hh * 512 + (q + 1) * 128],
                                wstr2[:, hh * D + 512: hh * D + 1024],
                                start=(hh == 0), stop=False))
                    p_pr = INC("tensor", lambda t, q=qc: t.matmul(
                        psPR[:, 512:1024], aoT[:, 7 * 512 + q * 128: 7 * 512 + (q + 1) * 128],
                        wstr2[:, 7 * D + 512: 7 * D + 1024],
                        start=False, stop=True))
                    WAIT("vector", cPE, p_pr)
                    WAIT("vector", cXL, s_x2)
                    v_o = INC("vector", lambda v, s2=sl: v.scalar_tensor_tensor(
                        outst[:, s2 * D:(s2 + 1) * D], psPR[:], 1.0,
                        xst[:, s2 * D:(s2 + 1) * D], ALU.bypass, ALU.add))
                    WAIT("tensor", cDV, v_o)
                    WAIT("sync", cDV, v_o)
                    s_o = INCC("sync", cOS, lambda sy, bb=b, tt=ti, s2=sl: sy.dma_start(
                        out[bb, tt * 128:(tt + 1) * 128, :], outst[:, s2 * D:(s2 + 1) * D]))
                    WAIT("vector", cOS, s_o)

            if "attpart" in dbg and b == 0:
                WAIT("sync", cSY, cSY.n)
                INC("sync", lambda sy: sy.dma_start(dbg["attpart"], out[0]), 16)

            barrier()
            if b == 0:
                w1cw = {}
                w2cw = {}
                nc._w1cw, nc._w2cw = w1cw, w2cw
            else:
                w1cw, w2cw = nc._w1cw, nc._w2cw
            xnf_guard = {0: 0, 1: 0}
            w2_guard = {0: 0, 1: 0}
            w2_sguard = {0: 0, 1: 0}
            w1_guard = 0
            for th in range(0 if SKIP_FFN else 2):
                for e in range(NE):
                    first = (b == 0 and th == 0)
                    if first:
                        WAIT("gpsimd", cPE, w1_guard)
                        WAIT("gpsimd", cW1C, cW1C.n)
                    if not first:
                        WAIT("sync", cPE, w1_guard)
                        WAIT("sync", cW1C, cW1C.n)
                    w1ch = cW1 if first else cW1s
                    for mc in range(8):
                        if first:
                            INCC("gpsimd", cW1, lambda g, ee=e, m=mc: g.dma_start(
                                w1sb[:, m * FF:(m + 1) * FF],
                                w1_in[ee, m * 128:(m + 1) * 128, :]))
                        else:
                            INCC("sync", cW1s, lambda sy, ee=e, m=mc: sy.dma_start(
                                w1sb[:, m * FF:(m + 1) * FF], w1c_d[ee, m]))
                    g_w1 = w1ch.n
                    if first:
                        WAIT("sync", cW1, g_w1)
                        for _unused in ():
                            pass
                        for mc in range(8):
                            INCC("sync", cW1C, lambda sy, ee=e, m=mc: sy.dma_start(
                                w1c_d[ee, m], w1sb[:, m * FF:(m + 1) * FF]))
                        w1cw[e] = cW1C.n
                    for ts in range(2):
                        tsub = th * 2 + ts
                        sl = ts
                        WAIT("sync", cPE, xnf_guard[sl])
                        cXF = (cXF0, cXF1)[sl]
                        s_xn = INCC("sync", cXF, lambda sy, bb=b, t2=tsub, s2=sl: sy.dma_start(
                            xnf[:, s2 * 4096:(s2 + 1) * 4096].rearrange("p (c n) -> p c n", n=512),
                            xnT_d[bb, t2].rearrange("c p n -> p c n")))
                        WAIT("tensor", w1ch, g_w1)
                        WAIT("tensor", cXF, s_xn)
                        gel = {}
                        for fc in range(32):
                            ps = (psA, psB, psR)[fc % 3]
                            if fc >= 3:
                                WAIT("tensor", cAC, gel[fc - 3])
                            for c in range(7):
                                EMIT("tensor", lambda t, f=fc, cc=c, p=ps, s2=sl: t.matmul(
                                    p[:, 0:512],
                                    w1sb[:, cc * FF + f * 128: cc * FF + (f + 1) * 128],
                                    xnf[:, s2 * 4096 + cc * 512: s2 * 4096 + (cc + 1) * 512],
                                    start=(cc == 0), stop=False))
                            p_1 = INC("tensor", lambda t, f=fc, p=ps, s2=sl: t.matmul(
                                p[:, 0:512], w1sb[:, 7 * FF + f * 128: 7 * FF + (f + 1) * 128],
                                xnf[:, s2 * 4096 + 7 * 512: s2 * 4096 + 8 * 512],
                                start=False, stop=True))
                            WAIT("scalar", cPE, p_1)
                            gel[fc] = INC("scalar", lambda s, f=fc, p=ps: s.activation(
                                hsb[:, f * 512:(f + 1) * 512], p[:, 0:512], AF.Gelu))
                        xnf_guard[sl] = cPE.n
                        h_done = cAC.n
                        if "hsb" in dbg and b == 0 and th == 0 and e == 0 and ts == 0:
                            WAIT("sync", cAC, h_done)
                            INC("sync", lambda sy: sy.dma_start(dbg["hsb"], hsb[:]), 16)
                        for ddh in range(2):
                            for fq in range(4):
                                w2sl = fq % 2
                                cW2 = (cW2a, cW2b)[w2sl]
                                cW2C = (cW2Ca, cW2Cb)[w2sl]
                                if first and ts == 0:
                                    WAIT("gpsimd", cPE, w2_guard[w2sl])
                                    WAIT("gpsimd", cW2C, w2_sguard[w2sl])
                                    g_w2 = INCC("gpsimd", cW2, lambda g, ee=e, d2=ddh, f2=fq, s3=w2sl: g.dma_start(
                                        w2buf[:, s3 * 4096:(s3 + 1) * 4096].rearrange(
                                            "p (c n) -> p c n", n=512),
                                        w2_in[ee, f2 * 1024:(f2 + 1) * 1024,
                                              d2 * 512:(d2 + 1) * 512].rearrange(
                                            "(c p) n -> p c n", p=128)))
                                    WAIT("sync", cW2, g_w2)
                                    s_w2 = INCC("sync", cW2C, lambda sy, ee=e, d2=ddh, f2=fq, s3=w2sl: sy.dma_start(
                                        w2c_d[ee, d2, f2 * 1024:(f2 + 1) * 1024, :].rearrange(
                                            "(c p) n -> p c n", p=128),
                                        w2buf[:, s3 * 4096:(s3 + 1) * 4096].rearrange(
                                            "p (c n) -> p c n", n=512)))
                                    w2_sguard[w2sl] = s_w2
                                    WAIT("tensor", cW2, g_w2)
                                else:
                                    cW2s = (cW2sa, cW2sb)[w2sl]
                                    wa, wb = w2cw.get(e, (0, 0))
                                    WAIT("sync", cW2Ca, wa)
                                    WAIT("sync", cW2Cb, wb)
                                    WAIT("sync", cPE, w2_guard[w2sl])
                                    g_w2 = INCC("sync", cW2s, lambda sy, ee=e, d2=ddh, f2=fq, s3=w2sl: sy.dma_start(
                                        w2buf[:, s3 * 4096:(s3 + 1) * 4096].rearrange(
                                            "p (c n) -> p c n", n=512),
                                        w2c_d[ee, d2, f2 * 1024:(f2 + 1) * 1024, :].rearrange(
                                            "(c p) n -> p c n", p=128)))
                                    WAIT("tensor", cW2s, g_w2)
                                WAIT("tensor", cAC, gel[fq * 8 + 7])
                                for tc in range(4):
                                    for fc in range(8):
                                        fg = fq * 8 + fc

                                        def p2mm(t, f2=fq, f=fc, t3=tc, s3=w2sl, fgv=fg):
                                            ysl = (psAt0[:, 0:512], psAt1[:, 0:512],
                                                   psPR[:, 0:512], psPR[:, 512:1024])[t3]
                                            return t.matmul(
                                                ysl,
                                                hsb[:, fgv * 512 + t3 * 128: fgv * 512 + (t3 + 1) * 128],
                                                w2buf[:, s3 * 4096 + f * 512: s3 * 4096 + (f + 1) * 512],
                                                start=(fgv == 0), stop=(fgv == 31),
                                                skip_group_check=True)
                                        if tc == 3 and fc == 7:
                                            INC("tensor", p2mm)
                                        else:
                                            EMIT("tensor", p2mm)
                                w2_guard[w2sl] = cPE.n
                            p_y = cPE.n
                            WAIT("vector", cPE, p_y)

                            def y_ev(v, d2=ddh, t2=tsub, t4=ts, ee=e):
                                last = None
                                for tc in range(4):
                                    tct = t2 * 4 + tc
                                    tca = t4 * 4 + tc
                                    ysl = (psAt0[:, 0:512], psAt1[:, 0:512],
                                           psPR[:, 0:512], psPR[:, 512:1024])[tc]
                                    asl = accsb[:, tca * D + d2 * 512: tca * D + d2 * 512 + 512]
                                    gcol = gates[:, tct * 9 + ee: tct * 9 + ee + 1]
                                    if ee == 0:
                                        last = v.tensor_scalar(asl, ysl, gcol, None, ALU.mult)
                                    else:
                                        last = v.scalar_tensor_tensor(asl, ysl, gcol, asl,
                                                                      ALU.mult, ALU.add)
                                return last
                            v_y = INC("vector", y_ev)
                            WAIT("tensor", cDV, v_y)
                        if first and ts == 0:
                            w2cw[e] = (cW2Ca.n, cW2Cb.n)
                        if ts == 1:
                            w1_guard = xnf_guard[sl]
                for tc8 in range(8):
                    tct = th * 8 + tc8
                    sl = tc8 % 2
                    s_ol = INCC("sync", cXL, lambda sy, bb=b, tt=tct, s2=sl: sy.dma_start(
                        xst[:, s2 * D:(s2 + 1) * D], out[bb, tt * 128:(tt + 1) * 128, :]))
                    s_xt = INCC("sync", cXT, lambda sy, bb=b, tt=tct, s2=sl: sy.dma_start(
                        xntok[:, s2 * D:(s2 + 1) * D], xntok_d[bb, tt * 128:(tt + 1) * 128, :]))
                    WAIT("vector", cXT, s_xt)
                    WAIT("vector", cXL, s_ol)

                    VSTEP(lambda v, t5=tct, t6=tc8, s2=sl: v.scalar_tensor_tensor(
                        outst[:, s2 * D:(s2 + 1) * D], xntok[:, s2 * D:(s2 + 1) * D],
                        gates[:, t5 * 9 + 8: t5 * 9 + 9], accsb[:, t6 * D:(t6 + 1) * D],
                        ALU.mult, ALU.add))
                    v_mg = INC("vector", lambda v, s2=sl: v.tensor_add(
                        outst[:, s2 * D:(s2 + 1) * D], outst[:, s2 * D:(s2 + 1) * D],
                        xst[:, s2 * D:(s2 + 1) * D]))
                    WAIT("sync", cDV, v_mg)
                    s_o2 = INCC("sync", cOS, lambda sy, bb=b, tt=tct, s2=sl: sy.dma_start(
                        out[bb, tt * 128:(tt + 1) * 128, :], outst[:, s2 * D:(s2 + 1) * D]))
                    WAIT("vector", cOS, s_o2)
            barrier()

        for c in chans:
            WAIT("sync", c, c.n)

        blk = es.enter_context(nc.Block())

        def mk(name):
            def body(e):
                for f in Q[name]:
                    f(e)
            return body
        blk.sync(mk("sync"))
        blk.gpsimd(mk("gpsimd"))
        blk.vector(mk("vector"))
        blk.scalar(mk("scalar"))
        blk.tensor(mk("tensor"))
    return nc


def kernel(L0_attn_x, L0_attn_W_qkv, L0_attn_W_o, L0_ffn_router_w, L0_ffn_router_b,
           L0_ffn_w1, L0_ffn_w2, _debug_names=(), _nc=None):
    x = np.ascontiguousarray(np.asarray(L0_attn_x))
    shards = [x[i * B_CORE:(i + 1) * B_CORE] for i in range(N_CORES)]
    nc = _nc if _nc is not None else build_program(_debug_names)
    base = {
        "L0_attn_W_qkv": np.ascontiguousarray(np.asarray(L0_attn_W_qkv)),
        "L0_attn_W_o": np.ascontiguousarray(np.asarray(L0_attn_W_o)),
        "L0_ffn_router_w": np.ascontiguousarray(np.asarray(L0_ffn_router_w)),
        "L0_ffn_router_b": np.ascontiguousarray(np.asarray(L0_ffn_router_b)),
        "L0_ffn_w1": np.ascontiguousarray(np.asarray(L0_ffn_w1)),
        "L0_ffn_w2": np.ascontiguousarray(np.asarray(L0_ffn_w2)),
    }
    in_maps = [dict(base, L0_attn_x=shards[i]) for i in range(N_CORES)]
    res = run_bass_kernel_spmd(nc, in_maps, core_ids=list(range(N_CORES)))
    outs = [np.asarray(res.results[i]["out"]) for i in range(N_CORES)]
    full = np.concatenate(outs, axis=0)
    if _debug_names:
        return full, res
    return full
